# version 13
# baseline (speedup 1.0000x reference)
import contextlib
import numpy as np
import concourse.bass as bass
import concourse.mybir as mybir
from concourse.bass_utils import run_bass_kernel_spmd

F32 = mybir.dt.float32
F32R = mybir.dt.float32r
BF16 = mybir.dt.bfloat16
AF = mybir.ActivationFunctionType
ALU = mybir.AluOpType
I16 = mybir.dt.int16
I32 = mybir.dt.int32

NCORE = 8
D = 2048
NB = 16
T = 1024
TW = 128
TT = T + TW
NTILE = TT // 128
H = 16
MODSH = 12288 // NCORE
HID = 16384
HIDP = 8192
GUSH = D // NCORE
PRSH = HIDP // NCORE
C_THR = 0.5009765625
LN_EPS = 1e-5
RMS_EPS = 1e-6
FFN_EPS = float(np.finfo(np.float32).eps)

DBG = ()
STAGE = 99
QSUB = 99
DBGLN = False


class Sem:
    __slots__ = ("h", "v")

    def __init__(self, st, nc, name):
        self.h = st.enter_context(nc.semaphore(name))
        self.v = 0

    def inc(self, instr, n=1):
        instr.then_inc(self.h, n)
        self.v += n
        return instr

    def dma(self, instr):
        return self.inc(instr, 16)

    def cc(self, instr):
        instr.then_inc(self.h)
        self.v += 1
        return instr


def build(dbg=(), stage=99):
    nc = bass.Bass("TRN2", target_bir_lowering=False)
    st = contextlib.ExitStack()
    nc._keep = st

    def inp(name, shape, dt=F32):
        return nc.dram_tensor(name, shape, dt, kind="ExternalInput").ap()

    xT = inp("xT", [D, TT])
    WqT = inp("WqT", [D, D])
    WkT = inp("WkT", [D, D])
    WvT = inp("WvT", [D, D])
    WoT = inp("WoT", [D, D])
    nwc = inp("nwc", [D, 1])
    wmod_sl = inp("wmod_sl", [D, MODSH])
    cc = inp("cc", [D, 1])
    wguT_sl = inp("wguT_sl", [GUSH, HID])
    wprT_sl = inp("wprT_sl", [PRSH, D])
    wmask = inp("wmask", [128, 1])
    u64d = inp("u64d", [128, 128])
    e2c = inp("e2c", [2, 128], BF16)
    e2b = inp("e2b", [2, 128], BF16)
    g64f = inp("g64f", [128, 2])
    g64b = inp("g64b", [128, 2], BF16)
    ones128b = inp("ones128b", [128, 1], BF16)
    ones1x128 = inp("ones1x128", [1, 128])
    identf = inp("identf", [128, 128])
    identb = inp("identb", [128, 128], BF16)
    masktn = inp("masktn", [128, 512], BF16)
    epsc = inp("epsc", [128, 4])

    out_fm = nc.dram_tensor("out_fm", [D, T], F32, kind="ExternalOutput").ap()

    dr = lambda n, s, dt=F32: nc.dram_tensor(n, s, dt)
    adsnap_d = None
    adsnap_d = dr("adsnap_d", [128, 64])
    wprsnap_d = dr("wprsnap_d", [HIDP, D], BF16)
    wgusnap_d = dr("wgusnap_d", [D, HID], BF16)
    modsh_d = dr("modsh_d", [1, MODSH])
    mod_d = dr("mod_d", [NCORE, MODSH])
    hp_d = dr("hp_d", [D, TT], BF16)
    q_d = dr("q_d", [TT, D], BF16)
    sigf_d = dr("sigf_d", [TT, D], F32)
    v_d = dr("v_d", [TT, D], BF16)
    o_d = dr("o_d", [D, T], F32)
    x2_d = dr("x2_d", [D, T], F32)
    h2_d = dr("h2_d", [D, T], BF16)
    qdT_d = dr("qdT_d", [128, 256], BF16)
    wgu_sh_d = dr("wgu_sh_d", [GUSH, HID], BF16)
    wpr_sh_d = dr("wpr_sh_d", [PRSH, D], BF16)
    wgu_d = dr("wgu_d", [D, HID], BF16)
    ccdum_d = dr("ccdum_d", [1, 128])
    ccdum2_d = dr("ccdum2_d", [NCORE, 128])
    wpr_d = dr("wpr_d", [HIDP, D], BF16)
    h_d = dr("h_d", [HIDP, T], BF16)
    dbgln_d = dr("dbgln_d", [128, 512], F32)

    sb = lambda n, s, dt: st.enter_context(nc.sbuf_tensor(n, s, dt))
    sem = lambda n: Sem(st, nc, n)

    arena = sb("arena", [128, 4096], F32)
    _b = nc.sbuf_base if not callable(nc.sbuf_base) else nc.sbuf_base()
    ar0 = _b - 16384
    wpanr = [nc.alloc_sbuf_tensor_at(f"wpanr{i}", [128, MODSH], F32R, offset=ar0 + i * 6144)
             for i in range(2)]
    f_t = nc.alloc_sbuf_tensor_at("f_t", [128, 2048], F32, offset=ar0)
    h2_sb = nc.alloc_sbuf_tensor_at("h2_sb", [128, NB * 512], BF16, offset=ar0)

    k_u64r = sb("k_u64r", [128, 128], F32R)
    k_g64r = sb("k_g64r", [128, 2], F32R)
    k_e2c = sb("k_e2c", [2, 128], BF16)
    k_e2b = sb("k_e2b", [2, 128], BF16)
    k_g64b = sb("k_g64b", [128, 2], BF16)
    k_o128b = sb("k_o128b", [128, 1], BF16)
    k_o1x128r = sb("k_o1x128r", [1, 128], F32R)
    k_identf = sb("k_identf", [128, 128], F32)
    k_identb = sb("k_identb", [128, 128], BF16)
    k_masktn = sb("k_masktn", [128, 512], BF16)
    k_wmask = sb("k_wmask", [128, 1], F32)
    k_eps = sb("k_eps", [128, 4], F32)
    k_nw = sb("k_nw", [128, 16], F32)
    k_c = sb("k_c", [128, 16], F32)
    silu_c = sb("silu_c", [128, 16], F32R)
    rows16 = sb("rows16", [16, 128 * 6], F32)
    mod_fm = sb("mod_fm", [128, 96], F32)
    ad_sa = sb("ad_sa", [128, 16], F32)
    ad_ca1 = sb("ad_ca1", [128, 16], F32)
    ad_ga = sb("ad_ga", [128, 16], F32)
    ad_sb = sb("ad_sb", [128, 16], F32)
    ad_cb1 = sb("ad_cb1", [128, 16], F32)
    ad_gb = sb("ad_gb", [128, 16], F32)
    sgn_s = sb("sgn_s", [128, 32], F32)
    mod_part = sb("mod_part", [1, MODSH], F32)

    qw_bf = [sb(f"qw_bf{i}", [128, 512], BF16) for i in range(2)]
    qwa = sb("qwa", [128, 512], BF16)
    qdense = sb("qdense", [128, 256], F32)
    qdbf = sb("qdbf", [128, 256], BF16)
    qdT = sb("qdT", [128, 256], BF16)
    qspair = sb("qspair", [2, 512], BF16)
    qsgn = sb("qsgn", [128, 512], BF16)
    qp = sb("qp", [128, 512], BF16)
    qhat = sb("qhat", [128, 512], BF16)
    qout = sb("qout", [128, 512], BF16)

    x_g = sb("x_g", [128, NB * 128], F32)
    xbf_g = sb("xbf_g", [128, NB * 128], BF16)
    xsq_g = sb("xsq_g", [128, NB * 128], BF16)
    u1_g = sb("u1_g", [128, NB * 128], F32)
    hp_g = sb("hp_g", [128, NB * 128], BF16)
    stat2 = sb("stat2", [128, 2], F32)
    mean_t = sb("mean_t", [128, 1], F32)
    var_t = sb("var_t", [128, 1], F32)
    var2_t = sb("var2_t", [128, 1], F32)
    rstd_t = sb("rstd_t", [128, 1], F32)
    nmr_t = sb("nmr_t", [128, 1], F32)
    row2 = sb("row2", [1, 256], F32R)
    hp_t = [sb(f"hp_t{i}", [128, NB * 128], BF16) for i in range(2)]
    wproj_pan = [sb(f"wproj_pan{i}", [128, 1024], BF16) for i in range(2)]
    wproj_pan2 = [sb(f"wproj_pan2{i}", [128, 1024], BF16) for i in range(2)]
    sf_ev = sb("sf_ev", [128, 2048], F32)
    q_p = sb("q_p", [128, 2048], BF16)
    v_p = sb("v_p", [128, 2048], BF16)
    sigf_p = sb("sigf_p", [128, 2048], F32)
    logf_p = sb("logf_p", [128, 2048], F32R)
    expA_p = sb("expA_p", [128, 2048], BF16)
    expnA_p = sb("expnA_p", [128, 2048], BF16)
    qdec_p = sb("qdec_p", [128, 2048], BF16)
    kdec_p = sb("kdec_p", [128, 2048], BF16)
    qdecT_p = sb("qdecT_p", [128, 2048], BF16)
    kdecT_p = sb("kdecT_p", [128, 2048], BF16)
    scoresT_p = sb("scoresT_p", [128, 1024], BF16)
    o_t = sb("o_t", [128, 1024], F32)
    d_all = sb("d_all", [128, NTILE * 32], F32)
    dsum_t = sb("dsum_t", [128, 32], F32)
    T_st = sb("T_st", [128, H * 128], F32)
    S_bf = sb("S_bf", [128, H * 128], BF16)
    onT_t = sb("onT_t", [128, NB * 128], BF16)
    wo_pan = [sb(f"wo_pan{i}", [128, 2048], BF16) for i in range(2)]
    gu_pan = [sb(f"gu_pan{i}", [128, 2048], BF16) for i in range(2)]
    gu_pan2 = [sb(f"gu_pan2{i}", [128, 2048], BF16) for i in range(2)]
    silu_t = sb("silu_t", [128, 512], BF16)
    h_t = sb("h_t", [128, 512], BF16)
    hsq_t = sb("hsq_t", [128, 512], BF16)
    ffs = sb("ffs", [128, 4], F32)
    ffrow = sb("ffrow", [1, 512], F32R)
    rstdffn_sb = sb("rstdffn_sb", [128, 512], F32)
    pr_pan = [sb(f"pr_pan{i}", [128, 1024], BF16) for i in range(2)]
    hproj_t = [sb(f"hproj_t{i}", [128, 512], BF16) for i in range(2)]
    x2_t = sb("x2_t", [128, 512], F32)
    out_t = sb("out_t", [128, 512], F32)

    pbig = st.enter_context(nc.psum_tensor("pbig", [128, 4096], F32))

    def pb(bank, off=0, w=512):
        return pbig[:, bank * 512 + off: bank * 512 + off + w]

    def pb1(bank, off, w):
        return pbig[0:1, bank * 512 + off: bank * 512 + off + w]

    pbig_bf = pbig.bitcast(BF16)

    def pbb(bank, off=0, w=1024):
        return pbig_bf[:, bank * 1024 + off: bank * 1024 + off + w]

    s_x = sem("s_x")
    s_g = sem("s_g")
    s_sq = sem("s_sq")
    s_t = sem("s_t")
    s_v = sem("s_v")
    s_s = sem("s_s")
    s_c = sem("s_c")
    s_go = sem("s_go")

    dbg_t = {}

    with nc.Block() as blk:
        def on(eng):
            return getattr(blk, eng)

        @on("sync")
        def _(e):
            s_x.dma(e.dma_start(out=k_e2c[:], in_=e2c[:, :]))
            s_x.dma(e.dma_start(out=k_e2b[:], in_=e2b[:, :]))
            s_x.dma(e.dma_start(out=k_g64b[:], in_=g64b[:, :]))
            s_x.dma(e.dma_start(out=k_o128b[:], in_=ones128b[:, :]))
            s_x.dma(e.dma_start(out=k_identf[:], in_=identf[:, :]))
            s_x.dma(e.dma_start(out=k_identb[:], in_=identb[:, :]))
            s_x.dma(e.dma_start(out=k_masktn[:], in_=masktn[:, :]))
            s_x.dma(e.dma_start(out=k_wmask[:], in_=wmask[:, :]))
            s_x.dma(e.dma_start(out=k_eps[:], in_=epsc[:, :]))
        v_kx = s_x.v

        @on("gpsimd")
        def _(e):
            s_g.dma(e.dma_start(out=k_u64r[:], in_=u64d[:, :]))
            s_g.dma(e.dma_start(out=k_g64r[:], in_=g64f[:, :]))
            s_g.dma(e.dma_start(out=k_o1x128r[:], in_=ones1x128[:, :]))
        v_kg = s_g.v

        @on("sync")
        def _(e):
            s_x.dma(e.dma_start(out=rows16[:, 0:128], in_=cc.rearrange("(a p) o -> a (p o)", a=16)[:, :]))
            s_x.dma(e.dma_start(out=rows16[:, 128:256], in_=nwc.rearrange("(a p) o -> a (p o)", a=16)[:, :]))
        v_cnw = s_x.v

        @on("tensor")
        def _(e):
            e.wait_ge(s_x.h, v_cnw)
            s_t.inc(e.transpose(pbig[0:128, 0:16], rows16[:, 0:128], k_identf[0:16, 0:16]))
            s_t.inc(e.transpose(pbig[0:128, 16:32], rows16[:, 128:256], k_identf[0:16, 0:16]))
        v_cnwT = s_t.v

        @on("vector")
        def _(e):
            e.wait_ge(s_t.h, v_cnwT)
            s_v.inc(e.tensor_copy(k_c[:], pbig[:, 0:16]))
            s_v.inc(e.tensor_copy(k_nw[:], pbig[:, 16:32]))
            e.wait_ge(s_v.h, s_v.v)
        v_cnwE = s_v.v

        @on("scalar")
        def _(e):
            e.wait_ge(s_v.h, v_cnwE)
            s_s.inc(e.activation(silu_c[:], k_c[:], AF.Silu))
        v_siluc = s_s.v

        wv_mod = wmod_sl.rearrange("(b p) m -> b p m", p=128)
        v_t_mod0 = s_t.v

        @on("gpsimd")
        def _(e):
            for k in range(NB):
                if k >= 2:
                    e.wait_ge(s_t.h, v_t_mod0 + 3 * (k - 1))
                s_g.dma(e.dma_start(out=wpanr[k % 2][:], in_=wv_mod[k, :, :]))
        v_modpan0 = v_kg + 16

        @on("tensor")
        def _(e):
            e.wait_ge(s_s.h, v_siluc)
            for k in range(NB):
                e.wait_ge(s_g.h, v_modpan0 + 16 * k)
                for j in range(3):
                    s_t.inc(e.matmul(pb1(j, 0, 512), silu_c[:, k:k + 1], wpanr[k % 2][:, j * 512:(j + 1) * 512],
                                     start=(k == 0), stop=(k == NB - 1)))
        v_t_mod = s_t.v

        @on("vector")
        def _(e):
            e.wait_ge(s_t.h, v_t_mod)
            for j in range(3):
                s_v.inc(e.tensor_copy(mod_part[:, j * 512:(j + 1) * 512], pb1(j, 0, 512)))
        v_v_mod = s_v.v

        @on("scalar")
        def _(e):
            e.wait_ge(s_v.h, v_v_mod)
            s_sq.dma(e.dma_start(out=modsh_d[:, :], in_=mod_part[:]))
        v_modst = s_sq.v

        @on("gpsimd")
        def _(e):
            e.wait_ge(s_sq.h, v_modst)
            s_c.cc(e.collective_compute(
                "AllGather", ALU.bypass, replica_groups=[list(range(NCORE))],
                ins=[modsh_d.ap().opt()], outs=[mod_d.ap().opt()]))
            e.wait_ge(s_c.h, s_c.v)
            mfl = mod_d.ap().rearrange("r m -> (r m)")
            mv16 = mfl.rearrange("(s a p) -> s a p", a=16, p=128)
            for sv in range(6):
                s_g.dma(e.dma_start(out=rows16[:, sv * 128:(sv + 1) * 128], in_=mv16[sv, :, :]))
        v_modfm = s_g.v

        @on("tensor")
        def _(e):
            e.wait_ge(s_g.h, v_modfm)
            for sv in range(6):
                s_t.inc(e.transpose(pbig[0:128, 64 + sv * 16: 64 + (sv + 1) * 16], rows16[:, sv * 128:(sv + 1) * 128],
                                    k_identf[0:16, 0:16]))
        v_modT = s_t.v

        @on("vector")
        def _(e):
            e.wait_ge(s_t.h, v_modT)
            s_v.inc(e.tensor_copy(mod_fm[:], pbig[:, 64:160]))
            e.wait_ge(s_v.h, s_v.v)
        v_modfm2 = s_v.v

        @on("scalar")
        def _(e):
            e.wait_ge(s_v.h, v_modfm2)
            s_s.inc(e.sign(sgn_s[:, 0:16], mod_fm[:, 32:48]))
            s_s.inc(e.sign(sgn_s[:, 16:32], mod_fm[:, 80:96]))
        v_sgnm = s_s.v

        @on("vector")
        def _(e):
            s_v.inc(e.tensor_copy(ad_sa[:], mod_fm[:, 0:16]))
            s_v.inc(e.tensor_scalar(ad_ca1[:], mod_fm[:, 16:32], 1.0, None, op0=ALU.add, op1=ALU.bypass))
            s_v.inc(e.tensor_copy(ad_sb[:], mod_fm[:, 48:64]))
            s_v.inc(e.tensor_scalar(ad_cb1[:], mod_fm[:, 64:80], 1.0, None, op0=ALU.add, op1=ALU.bypass))
            e.wait_ge(s_s.h, v_sgnm)
            s_v.inc(e.tensor_scalar(ad_ga[:].bitcast(I32), mod_fm[:, 32:48].bitcast(I32), 0x7FFFFFFF, None,
                                    op0=ALU.bitwise_and, op1=ALU.bypass))
            e.wait_ge(s_v.h, s_v.v)
            s_v.inc(e.tensor_scalar(ad_ga[:], ad_ga[:], 0.25, None, op0=ALU.max, op1=ALU.bypass))
            e.wait_ge(s_v.h, s_v.v)
            s_v.inc(e.tensor_mul(ad_ga[:], ad_ga[:], sgn_s[:, 0:16]))
            s_v.inc(e.tensor_scalar(ad_gb[:].bitcast(I32), mod_fm[:, 80:96].bitcast(I32), 0x7FFFFFFF, None,
                                    op0=ALU.bitwise_and, op1=ALU.bypass))
            e.wait_ge(s_v.h, s_v.v)
            s_v.inc(e.tensor_scalar(ad_gb[:], ad_gb[:], 0.25, None, op0=ALU.max, op1=ALU.bypass))
            e.wait_ge(s_v.h, s_v.v)
            s_v.inc(e.tensor_mul(ad_gb[:], ad_gb[:], sgn_s[:, 16:32]))
            e.wait_ge(s_v.h, s_v.v)
        v_adaln = s_v.v

        if stage >= 2:
            qblocks = [("gu", b, wguT_sl, wgu_sh_d, HID) for b in range(GUSH // 128)] + \
                      [("pr", b, wprT_sl, wpr_sh_d, D) for b in range(PRSH // 128)]
            for (tag, qb, src, dst, width) in qblocks:
                ntl = width // 512
                srcv = src.rearrange("(a p) m -> a p m", p=128)
                dstv = dst.rearrange("(a p) m -> a p m", p=128)

                v_t0 = s_t.v
                v_v0 = s_v.v
                for j in range(ntl):
                    @on("gpsimd")
                    def _(e, j=j):
                        e.wait_ge(s_v.h, s_v.v)
                        e.wait_ge(s_t.h, s_t.v)
                        s_g.dma(e.dma_start(out=qw_bf[j % 2][:], in_=srcv[qb, :, j * 512:(j + 1) * 512]))
                    v_w = s_g.v

                    @on("vector")
                    def _(e, j=j, v_w=v_w):
                        e.wait_ge(s_g.h, v_w)
                        e.wait_ge(s_t.h, s_t.v)
                        s_v.inc(e.tensor_scalar(qwa[:].bitcast(I16), qw_bf[j % 2][:].bitcast(I16), 0x7FFF, None,
                                                op0=ALU.bitwise_and, op1=ALU.bypass))
                        e.wait_ge(s_v.h, s_v.v)
                    v_wa = s_v.v

                    @on("tensor")
                    def _(e, j=j, v_wa=v_wa):
                        e.wait_ge(s_v.h, v_wa)
                        for qr in range(4):
                            sl = qwa[:, qr * 128:(qr + 1) * 128]
                            s_t.inc(e.matmul(pb(6, 4 * j + qr, 1), sl, k_g64b[:, 0:1], start=True, stop=True))
                            s_t.inc(e.matmul(pb(6, 128 + 4 * j + qr, 1), sl, k_g64b[:, 1:2], start=True, stop=True))
                v_sums = s_t.v

                @on("vector")
                def _(e):
                    e.wait_ge(s_t.h, v_sums)
                    s_v.inc(e.tensor_scalar(qdense[:], pb(6, 0, 256), 1.0 / 64.0, None, op0=ALU.mult, op1=ALU.bypass))
                    e.wait_ge(s_v.h, s_v.v)
                    s_v.inc(e.tensor_copy(qdbf[:], qdense[:]))
                    e.wait_ge(s_v.h, s_v.v)
                    s_v.inc(e.tensor_scalar(qdbf[:], qdbf[:], 1e-8, None, op0=ALU.max, op1=ALU.bypass))
                    e.wait_ge(s_v.h, s_v.v)
                v_dense = s_v.v

                @on("tensor")
                def _(e):
                    e.wait_ge(s_v.h, v_dense)
                    s_t.inc(e.transpose(pbb(7, 0, 256)[:, 0:128], qdbf[:, 0:128], k_identb[:]))
                    s_t.inc(e.transpose(pbb(7, 0, 256)[:, 128:256], qdbf[:, 128:256], k_identb[:]))
                v_dT = s_t.v

                @on("vector")
                def _(e):
                    e.wait_ge(s_t.h, v_dT)
                    s_v.inc(e.tensor_copy(qdT[:], pbb(7, 0, 256)))
                v_dTe = s_v.v

                @on("scalar")
                def _(e):
                    e.wait_ge(s_v.h, v_dTe)
                    s_sq.dma(e.dma_start(out=qdT_d[:, :], in_=qdT[:, :]))
                v_dTd = s_sq.v

                for j in range(ntl if QSUB >= 2 else 0):
                    @on("scalar")
                    def _(e, j=j, v_dTd=v_dTd):
                        e.wait_ge(s_sq.h, v_dTd)
                        e.wait_ge(s_t.h, s_t.v)
                        e.wait_ge(s_v.h, s_v.v)
                        flat = qdT_d.ap().rearrange("r m -> (r m)")
                        src_ap = flat[(4 * j) * 256:(4 * j + 4) * 256].rearrange(
                            "(q h m) -> h q m", q=4, h=2)
                        s_sq.dma(e.dma_start(out=qspair[:, :].rearrange("h (q m) -> h q m", q=4), in_=src_ap))
                    v_sp = s_sq.v

                    @on("tensor")
                    def _(e, v_sp=v_sp):
                        e.wait_ge(s_sq.h, v_sp)
                        s_t.inc(e.matmul(pb(7), k_e2c[:, :], qspair[:, :], start=True, stop=True))
                        s_t.inc(e.matmul(pb(5), k_e2b[:, :], qspair[:, :], start=True, stop=True))
                    v_bc = s_t.v

                    if QSUB < 3:
                        continue
                    @on("gpsimd")
                    def _(e, j=j):
                        e.wait_ge(s_v.h, s_v.v)
                        s_g.dma(e.dma_start(out=qw_bf[j % 2][:], in_=srcv[qb, :, j * 512:(j + 1) * 512]))
                    v_w2 = s_g.v

                    @on("scalar")
                    def _(e, j=j, v_w2=v_w2):
                        e.wait_ge(s_g.h, v_w2)
                        e.wait_ge(s_v.h, s_v.v)
                        s_s.inc(e.sign(qsgn[:], qw_bf[j % 2][:]))
                    v_sgn2 = s_s.v

                    @on("vector")
                    def _(e, j=j, v_w2=v_w2, v_bc=v_bc, v_sgn2=v_sgn2):
                        e.wait_ge(s_g.h, v_w2)
                        s_v.inc(e.tensor_scalar(qwa[:].bitcast(I16), qw_bf[j % 2][:].bitcast(I16), 0x7FFF, None,
                                                op0=ALU.bitwise_and, op1=ALU.bypass))
                        if QSUB >= 4:
                            e.wait_ge(s_t.h, v_bc)
                            s_v.inc(e.tensor_tensor(qp[:], qwa[:], pb(7), op=ALU.is_gt))
                            e.wait_ge(s_s.h, v_sgn2)
                            s_v.inc(e.tensor_mul(qp[:], qp[:], qsgn[:]))
                        if QSUB >= 5:
                            s_v.inc(e.scalar_tensor_tensor(qhat[:], qp[:], 1.0, pb(5), op0=ALU.mult, op1=ALU.mult))
                        if QSUB >= 5:
                            s_v.inc(e.tensor_sub(qout[:], qhat[:], qw_bf[j % 2][:]))
                            s_v.inc(e.tensor_add(qout[:], qw_bf[j % 2][:], qout[:]))
                    v_q = s_v.v

                    if QSUB >= 6:
                        @on("gpsimd")
                        def _(e, j=j, v_q=v_q):
                            e.wait_ge(s_v.h, v_q)
                            s_go.dma(e.dma_start(out=dstv[qb, :, j * 512:(j + 1) * 512], in_=qout[:]))

            @on("gpsimd")
            def _(e):
                e.wait_ge(s_go.h, s_go.v)
                if QSUB < 7:
                    return
                s_c.cc(e.collective_compute(
                    "AllGather", ALU.bypass, replica_groups=[list(range(NCORE))],
                    ins=[wgu_sh_d.ap().opt()], outs=[wgu_d.ap().opt()]))
                s_c.cc(e.collective_compute(
                    "AllGather", ALU.bypass, replica_groups=[list(range(NCORE))],
                    ins=[wpr_sh_d.ap().opt()], outs=[wpr_d.ap().opt()]))
                s_c.cc(e.collective_compute(
                    "AllGather", ALU.bypass, replica_groups=[list(range(NCORE))],
                    ins=[ccdum_d.ap().opt()], outs=[ccdum2_d.ap().opt()]))
            v_wq_gathered = s_c.v

        @on("sync")
        def _(e):
            e.wait_ge(s_v.h, s_v.v)
            s_x.dma(e.dma_start(out=adsnap_d[0:128, 0:16], in_=ad_cb1[:]))

        xTv = xT.rearrange("(b p) t -> b p t", p=128)

        def emit_ln(src_fm_f32, src_fm_bf, stride, coloff, ca1, sa_, eps_col, gammav,
                    dst_sb, dst_dram, dst_col, sub_mean):
            @on("tensor")
            def _(e):
                e.wait_ge(s_v.h, s_v.v)

            @on("scalar")
            def _(e):
                e.wait_ge(s_t.h, s_t.v)
                for b in range(NB):
                    s_s.inc(e.square(xsq_g[:, b * 128:(b + 1) * 128],
                                     src_fm_bf[:, b * stride + coloff: b * stride + coloff + 128]))
            v_sq = s_s.v

            @on("tensor")
            def _(e):
                if sub_mean:
                    for b in range(NB):
                        s_t.inc(e.matmul(pb(6, 0, 1), src_fm_bf[:, b * stride + coloff: b * stride + coloff + 128],
                                         k_o128b[:, :], start=(b == 0), stop=(b == NB - 1)))
                e.wait_ge(s_s.h, v_sq)
                for b in range(NB):
                    s_t.inc(e.matmul(pb(6, 1, 1), xsq_g[:, b * 128:(b + 1) * 128],
                                     k_o128b[:, :], start=(b == 0), stop=(b == NB - 1)))
            v_st = s_t.v

            @on("vector")
            def _(e):
                e.wait_ge(s_t.h, v_st)
                s_v.inc(e.tensor_copy(stat2[:], pb(6, 0, 2)))
                e.wait_ge(s_v.h, s_v.v)
                if sub_mean:
                    s_v.inc(e.tensor_scalar(mean_t[:], stat2[:, 0:1], 1.0 / D, None, op0=ALU.mult, op1=ALU.bypass))
                    s_v.inc(e.tensor_scalar(var_t[:], stat2[:, 1:2], 1.0 / D, None, op0=ALU.mult, op1=ALU.bypass))
                    e.wait_ge(s_v.h, s_v.v)
                    s_v.inc(e.scalar_tensor_tensor(var2_t[:], mean_t[:], mean_t[:, 0:1], var_t[:],
                                                   op0=ALU.mult, op1=ALU.subtract))
                else:
                    s_v.inc(e.tensor_scalar(var2_t[:], stat2[:, 1:2], -1.0 / D, None, op0=ALU.mult, op1=ALU.bypass))
                e.wait_ge(s_v.h, s_v.v)
            v_vv = s_v.v

            @on("scalar")
            def _(e):
                e.wait_ge(s_v.h, v_vv)
                s_s.inc(e.activation(nmr_t[:], var2_t[:], AF.Ln, bias=k_eps[:, eps_col:eps_col + 1], scale=-1.0))
                e.wait_ge(s_s.h, s_s.v)
                s_s.inc(e.activation(rstd_t[:], nmr_t[:], AF.Exp, scale=-0.5))
            v_rs = s_s.v

            @on("vector")
            def _(e):
                e.wait_ge(s_s.h, v_rs)
                if sub_mean:
                    s_v.inc(e.scalar_tensor_tensor(nmr_t[:], mean_t[:], -1.0, rstd_t[:], op0=ALU.mult, op1=ALU.mult))
                else:
                    s_v.inc(e.memset(nmr_t[:], 0.0))
                e.wait_ge(s_v.h, s_v.v)
            v_rr = s_v.v

            @on("tensor")
            def _(e):
                e.wait_ge(s_v.h, v_rr)
                s_t.inc(e.transpose(pb1(7, 0, 128), rstd_t[:], k_identf[:]))
                s_t.inc(e.transpose(pb1(7, 128, 128), nmr_t[:], k_identf[:]))
            v_tr = s_t.v

            @on("vector")
            def _(e):
                e.wait_ge(s_t.h, v_tr)
                s_v.inc(e.tensor_copy(row2[:], pb1(7, 0, 256)))
            v_r2 = s_v.v

            @on("tensor")
            def _(e):
                e.wait_ge(s_v.h, v_r2)
                s_t.inc(e.matmul(pb(6, 0, 128), k_o1x128r[:, :], row2[:, 0:128], start=True, stop=True))
                s_t.inc(e.matmul(pb(7, 0, 128), k_o1x128r[:, :], row2[:, 128:256], start=True, stop=True))
            v_bc2 = s_t.v

            if DBGLN and dst_col == 0 and dst_dram is not None:
                @on("vector")
                def _(e):
                    e.wait_ge(s_t.h, v_bc2)
                    s_v.inc(e.tensor_copy(u1_g[:, 1920:2048], pb(6, 0, 128)))
                    s_v.inc(e.tensor_copy(u1_g[:, 1792:1920], pb(7, 0, 128)))
                    s_v.inc(e.tensor_copy(u1_g[:, 1780:1782], stat2[:, 0:2]))
                    s_v.inc(e.tensor_copy(u1_g[:, 1782:1783], mean_t[:]))
                    s_v.inc(e.tensor_copy(u1_g[:, 1783:1784], var2_t[:]))
                    s_v.inc(e.tensor_copy(u1_g[:, 1784:1785], rstd_t[:]))
                    s_v.inc(e.tensor_copy(u1_g[:, 1785:1786], nmr_t[:]))
                vdbg = s_v.v

                @on("sync")
                def _(e):
                    e.wait_ge(s_v.h, vdbg)
                    s_x.dma(e.dma_start(out=dbgln_d[:, 0:268], in_=u1_g[:, 1780:2048]))

            @on("vector")
            def _(e):
                e.wait_ge(s_t.h, v_bc2)
                for b in range(NB):
                    so = b * stride + coloff
                    s_v.inc(e.tensor_mul(u1_g[:, b * 128:(b + 1) * 128],
                                         src_fm_f32[:, so:so + 128], pb(6, 0, 128)))
                    e.wait_ge(s_v.h, s_v.v)
                    s_v.inc(e.tensor_add(u1_g[:, b * 128:(b + 1) * 128],
                                         u1_g[:, b * 128:(b + 1) * 128], pb(7, 0, 128)))
                    e.wait_ge(s_v.h, s_v.v)
                    if gammav is not None and ca1 is None:
                        s_v.inc(e.tensor_scalar(dst_sb[:, b * 128:(b + 1) * 128], u1_g[:, b * 128:(b + 1) * 128],
                                                gammav[:, b:b + 1], None, op0=ALU.mult, op1=ALU.bypass))
                        continue
                    if gammav is not None:
                        s_v.inc(e.tensor_scalar(u1_g[:, b * 128:(b + 1) * 128], u1_g[:, b * 128:(b + 1) * 128],
                                                gammav[:, b:b + 1], None, op0=ALU.mult, op1=ALU.bypass))
                        e.wait_ge(s_v.h, s_v.v)
                    s_v.inc(e.tensor_scalar(dst_sb[:, b * 128:(b + 1) * 128], u1_g[:, b * 128:(b + 1) * 128],
                                            ca1[:, b:b + 1], sa_[:, b:b + 1], op0=ALU.mult, op1=ALU.add))
            v_aff = s_v.v

            if dst_dram is not None:
                dv = dst_dram.rearrange("(b p) t -> b p t", p=128)

                @on("sync")
                def _(e):
                    e.wait_ge(s_v.h, v_aff)
                    for b in range(NB):
                        s_x.dma(e.dma_start(out=dv[b, :, dst_col:dst_col + 128],
                                            in_=dst_sb[:, b * 128:(b + 1) * 128]))
            return v_aff

        if stage >= 3:
            for tt in range(NTILE):
                @on("sync")
                def _(e, tt=tt):
                    e.wait_ge(s_v.h, s_v.v)
                    for b in range(NB):
                        s_x.dma(e.dma_start(out=x_g[:, b * 128:(b + 1) * 128],
                                            in_=xTv[b, :, tt * 128:(tt + 1) * 128]))
                v_xl = s_x.v

                @on("gpsimd")
                def _(e, tt=tt):
                    e.wait_ge(s_t.h, s_t.v)
                    for b in range(NB):
                        s_g.dma(e.dma_start(out=xbf_g[:, b * 128:(b + 1) * 128],
                                            in_=xTv[b, :, tt * 128:(tt + 1) * 128]))
                v_xbl = s_g.v

                @on("scalar")
                def _(e, v_xbl=v_xbl):
                    e.wait_ge(s_g.h, v_xbl)

                @on("vector")
                def _(e, v_xl=v_xl, v_xbl=v_xbl):
                    e.wait_ge(s_x.h, v_xl)
                    e.wait_ge(s_g.h, v_xbl)

                @on("tensor")
                def _(e, v_xbl=v_xbl):
                    e.wait_ge(s_g.h, v_xbl)
                emit_ln(x_g, xbf_g, 128, 0, ad_ca1, ad_sa, 0, None, hp_g, hp_d, tt * 128, True)

        @on("sync")
        def _(e):
            e.wait_ge(s_v.h, s_v.v)
            e.wait_ge(s_s.h, s_s.v)
            s_x.dma(e.dma_start(out=adsnap_d[0:128, 16:32], in_=ad_cb1[:]))

        if stage >= 4:
            hp_dv = hp_d.rearrange("(b p) t -> b p t", p=128)
            prj_ev = hp_g

            for tt in range(NTILE):
                @on("scalar")
                def _(e, tt=tt):
                    e.wait_ge(s_t.h, s_t.v)
                    e.wait_ge(s_x.h, s_x.v)
                    for b in range(NB):
                        s_sq.dma(e.dma_start(out=hp_t[tt % 2][:, b * 128:(b + 1) * 128],
                                             in_=hp_dv[b, :, tt * 128:(tt + 1) * 128]))
                v_hp = s_sq.v

                for ip, (Wt, kind) in enumerate(((WqT, "q"), (WkT, "k"), (WvT, "v"))):
                    Wv_ = Wt.rearrange("(b p) m -> b p m", p=128)
                    v_t0p = s_t.v
                    v_pan = []
                    for k in range(NB):
                        @on("gpsimd")
                        def _(e, k=k, v_t0p=v_t0p):
                            if k >= 2:
                                e.wait_ge(s_t.h, v_t0p + 4 * (k - 1))
                            else:
                                e.wait_ge(s_t.h, v_t0p)
                            s_g.dma(e.dma_start(out=wproj_pan[k % 2][:], in_=Wv_[k, :, 0:1024]))
                            s_g.dma(e.dma_start(out=wproj_pan2[k % 2][:], in_=Wv_[k, :, 1024:2048]))
                        v_pan.append(s_g.v)

                    @on("tensor")
                    def _(e, v_hp=v_hp, v_pan=v_pan, tt=tt):
                        e.wait_ge(s_sq.h, v_hp)
                        e.wait_ge(s_v.h, s_v.v)
                        e.wait_ge(s_s.h, s_s.v)
                        for k in range(NB):
                            e.wait_ge(s_g.h, v_pan[k])
                            lt = hp_t[tt % 2][:, k * 128:(k + 1) * 128]
                            st0, sp0 = (k == 0), (k == NB - 1)
                            e_ = e.matmul(pb(0), lt, wproj_pan[k % 2][:, 0:512], start=st0, stop=sp0)
                            s_t.inc(e_)
                            s_t.inc(e.matmul(pb(1), lt, wproj_pan[k % 2][:, 512:1024], start=st0, stop=sp0))
                            s_t.inc(e.matmul(pb(2), lt, wproj_pan2[k % 2][:, 0:512], start=st0, stop=sp0))
                            s_t.inc(e.matmul(pb(3), lt, wproj_pan2[k % 2][:, 512:1024], start=st0, stop=sp0))
                    v_mm = s_t.v

                    if kind == "q":
                        @on("scalar")
                        def _(e, v_mm=v_mm):
                            e.wait_ge(s_t.h, v_mm)
                            e.wait_ge(s_x.h, s_x.v)
                            for n in range(4):
                                s_s.inc(e.activation(prj_ev[:, n * 512:(n + 1) * 512], pb(n), AF.Silu))
                        v_ev = s_s.v

                        @on("sync")
                        def _(e, v_ev=v_ev, tt=tt):
                            e.wait_ge(s_s.h, v_ev)
                            s_x.dma(e.dma_start(out=q_d[tt * 128:(tt + 1) * 128, :], in_=prj_ev[:]))
                    elif kind == "k":
                        @on("scalar")
                        def _(e, v_mm=v_mm):
                            e.wait_ge(s_t.h, v_mm)
                            e.wait_ge(s_x.h, s_x.v)
                            for n in range(4):
                                s_s.inc(e.activation(f_t[:, n * 512:(n + 1) * 512], pb(n), AF.Sigmoid))
                            e.wait_ge(s_s.h, s_s.v)
                            for n in range(4):
                                s_s.inc(e.activation(sf_ev[:, n * 512:(n + 1) * 512],
                                                     f_t[:, n * 512:(n + 1) * 512], AF.Sigmoid))
                        v_ev = s_s.v

                        @on("sync")
                        def _(e, v_ev=v_ev, tt=tt):
                            e.wait_ge(s_s.h, v_ev)
                            s_x.dma(e.dma_start(out=sigf_d[tt * 128:(tt + 1) * 128, :], in_=sf_ev[:]))
                    else:
                        @on("vector")
                        def _(e, v_mm=v_mm, tt=tt):
                            e.wait_ge(s_t.h, v_mm)
                            e.wait_ge(s_x.h, s_x.v)
                            for n in range(4):
                                if tt == 0:
                                    s_v.inc(e.tensor_scalar(prj_ev[:, n * 512:(n + 1) * 512], pb(n),
                                                            k_wmask[:, 0:1], None, op0=ALU.mult, op1=ALU.bypass))
                                else:
                                    s_v.inc(e.tensor_copy(prj_ev[:, n * 512:(n + 1) * 512], pb(n)))
                        v_ev = s_v.v

                        @on("sync")
                        def _(e, v_ev=v_ev, tt=tt):
                            e.wait_ge(s_v.h, v_ev)
                            s_x.dma(e.dma_start(out=v_d[tt * 128:(tt + 1) * 128, :], in_=prj_ev[:]))

        @on("sync")
        def _(e):
            e.wait_ge(s_v.h, s_v.v)
            e.wait_ge(s_s.h, s_s.v)
            s_x.dma(e.dma_start(out=adsnap_d[0:128, 32:48], in_=ad_cb1[:]))

        if stage >= 5:
            @on("gpsimd")
            def _(e):
                e.memset(T_st[:], 0.0)
                s_g.inc(e.nop())
            v_Tz = s_g.v

            @on("vector")
            def _(e):
                e.wait_ge(s_g.h, v_Tz)

            o_dv = o_d.rearrange("(b p) t -> b p t", p=128)
            for p_ in range(NTILE):
                @on("scalar")
                def _(e, p_=p_):
                    e.wait_ge(s_x.h, s_x.v)
                    e.wait_ge(s_t.h, s_t.v)
                    e.wait_ge(s_v.h, s_v.v)
                    s_sq.dma(e.dma_start(out=q_p[:], in_=q_d[p_ * 128:(p_ + 1) * 128, :]))
                    s_sq.dma(e.dma_start(out=v_p[:], in_=v_d[p_ * 128:(p_ + 1) * 128, :]))
                    s_sq.dma(e.dma_start(out=sigf_p[:], in_=sigf_d[p_ * 128:(p_ + 1) * 128, :]))
                v_ld = s_sq.v

                @on("scalar")
                def _(e, v_ld=v_ld):
                    e.wait_ge(s_sq.h, v_ld)
                    s_s.inc(e.activation(logf_p[:], sigf_p[:], AF.Ln))
                v_lf = s_s.v

                @on("tensor")
                def _(e, v_lf=v_lf):
                    e.wait_ge(s_s.h, v_lf)
                    e.wait_ge(s_v.h, s_v.v)
                    for j in range(4):
                        s_t.inc(e.matmul(pb(j), k_u64r[:, :], logf_p[:, j * 512:(j + 1) * 512],
                                         start=True, stop=True))
                    for b in range(NB):
                        s_t.inc(e.matmul(pb(4, 2 * b, 2), logf_p[:, b * 128:(b + 1) * 128],
                                         k_g64r[:, :], start=True, stop=True))
                v_Amm = s_t.v

                @on("scalar")
                def _(e, v_Amm=v_Amm):
                    e.wait_ge(s_t.h, v_Amm)
                    for j in range(4):
                        s_s.inc(e.activation(expA_p[:, j * 512:(j + 1) * 512], pb(j), AF.Exp))
                    for j in range(4):
                        s_s.inc(e.activation(expnA_p[:, j * 512:(j + 1) * 512], pb(j), AF.Exp, scale=-1.0))
                v_exps = s_s.v

                @on("vector")
                def _(e, v_Amm=v_Amm, p_=p_):
                    e.wait_ge(s_t.h, v_Amm)
                    s_v.inc(e.tensor_copy(dsum_t[:], pb(4, 0, 32)))
                v_ds = s_v.v

                @on("scalar")
                def _(e, v_ds=v_ds, p_=p_):
                    e.wait_ge(s_v.h, v_ds)
                    s_s.inc(e.activation(d_all[:, p_ * 32:(p_ + 1) * 32], dsum_t[:], AF.Exp))
                v_dex = s_s.v

                @on("vector")
                def _(e, v_exps=v_exps, v_ld=v_ld):
                    e.wait_ge(s_s.h, v_exps)
                    s_v.inc(e.tensor_mul(qdec_p[:], q_p[:], expA_p[:]))
                    s_v.inc(e.scalar_tensor_tensor(kdec_p[:], sigf_p[:], 1.0, expnA_p[:],
                                                   op0=ALU.subtract, op1=ALU.mult))
                v_dec = s_v.v

                for (srcT, dstT) in ((qdec_p, qdecT_p), (kdec_p, kdecT_p)):
                    for half in range(2):
                        @on("tensor")
                        def _(e, srcT=srcT, half=half, v_dec=v_dec):
                            e.wait_ge(s_v.h, v_dec)
                            e.wait_ge(s_v.h, s_v.v)
                            for i in range(8):
                                b = half * 8 + i
                                s_t.inc(e.transpose(pbb(5, i * 128, 128), srcT[:, b * 128:(b + 1) * 128],
                                                    k_identb[:]))
                        v_tf = s_t.v

                        @on("vector")
                        def _(e, dstT=dstT, half=half, v_tf=v_tf):
                            e.wait_ge(s_t.h, v_tf)
                            s_v.inc(e.tensor_copy(dstT[:, half * 1024:(half + 1) * 1024], pbb(5, 0, 1024)))
                v_trdone = s_v.v

                for c_ in range(2):
                    cg = 2 * p_ + c_
                    par = 64 * c_
                    real = (p_ > 0)
                    @on("vector")
                    def _(e, cg=cg, v_dex=v_dex):
                        e.wait_ge(s_v.h, s_v.v)
                        if cg > 0:
                            e.wait_ge(s_s.h, s_s.v)
                        for h in range(H):
                            if cg == 0:
                                s_v.inc(e.tensor_scalar(S_bf[:, h * 128:(h + 1) * 128],
                                                        T_st[:, h * 128:(h + 1) * 128],
                                                        1.0, None, op0=ALU.mult, op1=ALU.bypass))
                            else:
                                pp0, pr0 = (cg - 1) // 2, (cg - 1) % 2
                                s_v.inc(e.tensor_scalar(S_bf[:, h * 128:(h + 1) * 128],
                                                        T_st[:, h * 128:(h + 1) * 128],
                                                        d_all[:, pp0 * 32 + 2 * h + pr0: pp0 * 32 + 2 * h + pr0 + 1],
                                                        None, op0=ALU.mult, op1=ALU.bypass))
                        e.wait_ge(s_v.h, s_v.v)
                    v_sbf = s_v.v

                    if real:
                        @on("tensor")
                        def _(e, par=par, v_trdone=v_trdone):
                            e.wait_ge(s_v.h, v_trdone)
                            for h in range(H):
                                bank = 6 + (h // 8)
                                s_t.inc(e.matmul(pbig[par:par + 64, bank * 512 + 64 * (h % 8): bank * 512 + 64 * (h % 8) + 64],
                                                 kdecT_p[:, h * 128 + par: h * 128 + par + 64],
                                                 qdecT_p[:, h * 128 + par: h * 128 + par + 64],
                                                 start=True, stop=True))
                        v_sc = s_t.v

                        @on("vector")
                        def _(e, par=par, v_sc=v_sc):
                            e.wait_ge(s_t.h, v_sc)
                            for n in range(2):
                                s_v.inc(e.tensor_mul(scoresT_p[par:par + 64, n * 512:(n + 1) * 512],
                                                     pbig[par:par + 64, (6 + n) * 512:(7 + n) * 512],
                                                     k_masktn[par:par + 64, :]))
                        v_msk = s_v.v

                        @on("tensor")
                        def _(e, par=par, v_msk=v_msk, v_sbf=v_sbf):
                            e.wait_ge(s_v.h, v_msk)
                            for h in range(H):
                                bank = h // 8
                                osl = pbig[:, bank * 512 + 64 * (h % 8): bank * 512 + 64 * (h % 8) + 64]
                                s_t.inc(e.matmul(osl, v_p[par:par + 64, h * 128:(h + 1) * 128],
                                                 scoresT_p[par:par + 64, 64 * h: 64 * h + 64],
                                                 start=True, stop=False))
                                s_t.inc(e.matmul(osl, S_bf[:, h * 128:(h + 1) * 128],
                                                 qdecT_p[:, h * 128 + par: h * 128 + par + 64],
                                                 start=False, stop=True))
                        v_omm = s_t.v

                    @on("tensor")
                    def _(e, par=par, v_trdone=v_trdone, v_sbf=v_sbf):
                        e.wait_ge(s_v.h, v_sbf)
                        for h in range(H):
                            bank = 2 + h // 4
                            s_t.inc(e.matmul(pbig[:, bank * 512 + 128 * (h % 4): bank * 512 + 128 * (h % 4) + 128],
                                             kdec_p[par:par + 64, h * 128:(h + 1) * 128],
                                             v_p[par:par + 64, h * 128:(h + 1) * 128],
                                             start=True, stop=True))
                    v_smm = s_t.v

                    @on("vector")
                    def _(e, cg=cg, v_smm=v_smm):
                        e.wait_ge(s_t.h, v_smm)
                        for h in range(H):
                            bank = 2 + h // 4
                            msl = pbig[:, bank * 512 + 128 * (h % 4): bank * 512 + 128 * (h % 4) + 128]
                            if cg == 0:
                                s_v.inc(e.scalar_tensor_tensor(T_st[:, h * 128:(h + 1) * 128],
                                                               T_st[:, h * 128:(h + 1) * 128], 1.0, msl,
                                                               op0=ALU.mult, op1=ALU.subtract))
                            else:
                                pp0, pr0 = (cg - 1) // 2, (cg - 1) % 2
                                s_v.inc(e.scalar_tensor_tensor(T_st[:, h * 128:(h + 1) * 128],
                                                               T_st[:, h * 128:(h + 1) * 128],
                                                               d_all[:, pp0 * 32 + 2 * h + pr0: pp0 * 32 + 2 * h + pr0 + 1],
                                                               msl, op0=ALU.mult, op1=ALU.subtract))
                        e.wait_ge(s_v.h, s_v.v)
                    v_tupd = s_v.v

                    if real:
                        @on("vector")
                        def _(e, v_omm=v_omm):
                            e.wait_ge(s_t.h, v_omm)
                            e.wait_ge(s_x.h, s_x.v)
                            s_v.inc(e.tensor_copy(o_t[:, 0:512], pb(0)))
                            s_v.inc(e.tensor_copy(o_t[:, 512:1024], pb(1)))
                        v_oev = s_v.v

                        @on("sync")
                        def _(e, p_=p_, c_=c_, v_oev=v_oev):
                            e.wait_ge(s_v.h, v_oev)
                            tok0 = 128 * p_ + 64 * c_ - TW
                            for h in range(H):
                                s_x.dma(e.dma_start(out=o_dv[h, :, tok0:tok0 + 64],
                                                    in_=o_t[:, 64 * h:64 * h + 64]))

        @on("sync")
        def _(e):
            e.wait_ge(s_v.h, s_v.v)
            e.wait_ge(s_s.h, s_s.v)
            s_x.dma(e.dma_start(out=adsnap_d[0:128, 48:64], in_=ad_cb1[:]))

        if stage >= 6:
            o_dv6 = o_d.rearrange("(b p) t -> b p t", p=128)
            x2_dv = x2_d.rearrange("(b p) t -> b p t", p=128)
            WoTv = WoT.rearrange("(b p) m -> b p m", p=128)
            x2f_g = sigf_p
            x2b_g = q_p

            for rt in range(8):
                @on("sync")
                def _(e, rt=rt):
                    e.wait_ge(s_v.h, s_v.v)
                    e.wait_ge(s_x.h, s_x.v)
                    for b in range(NB):
                        s_x.dma(e.dma_start(out=x_g[:, b * 128:(b + 1) * 128],
                                            in_=o_dv6[b, :, rt * 128:(rt + 1) * 128]))
                v_ol = s_x.v

                @on("gpsimd")
                def _(e, rt=rt):
                    e.wait_ge(s_t.h, s_t.v)
                    e.wait_ge(s_x.h, s_x.v)
                    for b in range(NB):
                        s_g.dma(e.dma_start(out=xbf_g[:, b * 128:(b + 1) * 128],
                                            in_=o_dv6[b, :, rt * 128:(rt + 1) * 128]))
                v_obl = s_g.v

                @on("scalar")
                def _(e, v_obl=v_obl):
                    e.wait_ge(s_g.h, v_obl)

                @on("vector")
                def _(e, v_ol=v_ol):
                    e.wait_ge(s_x.h, v_ol)

                emit_ln(x_g, xbf_g, 128, 0, None, None, 1, k_nw, onT_t, None, 0, False)
                v_on = s_v.v

                v_t0w = s_t.v
                for f in range(NB):
                    @on("gpsimd")
                    def _(e, f=f, v_t0w=v_t0w):
                        if f >= 2:
                            e.wait_ge(s_t.h, v_t0w + 16 * (f - 1))
                        else:
                            e.wait_ge(s_t.h, v_t0w)
                        s_g.dma(e.dma_start(out=wo_pan[f % 2][:], in_=WoTv[f, :, :]))
                    v_wp = s_g.v

                    @on("tensor")
                    def _(e, f=f, v_wp=v_wp, v_on=v_on):
                        e.wait_ge(s_g.h, v_wp)
                        if f == 0:
                            e.wait_ge(s_v.h, s_v.v)
                        for db in range(NB):
                            s_t.inc(e.matmul(pbig[:, db * 128:(db + 1) * 128],
                                             wo_pan[f % 2][:, db * 128:(db + 1) * 128],
                                             onT_t[:, f * 128:(f + 1) * 128],
                                             start=(f == 0 and db % 4 == 0), stop=(f == NB - 1),
                                             skip_group_check=True))
                v_wmm = s_t.v

                @on("sync")
                def _(e, rt=rt):
                    e.wait_ge(s_v.h, s_v.v)
                    for b in range(NB):
                        s_x.dma(e.dma_start(out=x_g[:, b * 128:(b + 1) * 128],
                                            in_=xTv[b, :, TW + rt * 128: TW + rt * 128 + 128]))
                v_xr = s_x.v

                @on("vector")
                def _(e, v_wmm=v_wmm, v_xr=v_xr):
                    e.wait_ge(s_t.h, v_wmm)
                    e.wait_ge(s_x.h, v_xr)
                    e.wait_ge(s_go.h, s_go.v)
                    for db in range(NB):
                        s_v.inc(e.scalar_tensor_tensor(x2f_g[:, db * 128:(db + 1) * 128],
                                                       pbig[:, db * 128:(db + 1) * 128],
                                                       ad_ga[:, db:db + 1], x_g[:, db * 128:(db + 1) * 128],
                                                       op0=ALU.mult, op1=ALU.add))
                    e.wait_ge(s_v.h, s_v.v)
                v_x2 = s_v.v

                @on("sync")
                def _(e, rt=rt, v_x2=v_x2):
                    e.wait_ge(s_v.h, v_x2)
                    for b in range(NB):
                        s_x.dma(e.dma_start(out=x2_dv[b, :, rt * 128:(rt + 1) * 128],
                                            in_=x2f_g[:, b * 128:(b + 1) * 128]))

                @on("gpsimd")
                def _(e, v_x2=v_x2):
                    e.wait_ge(s_v.h, v_x2)
                    s_go.dma(e.dma_start(out=x2b_g[:], in_=x2f_g[:]))
                v_x2b = s_go.v

                @on("scalar")
                def _(e, v_x2b=v_x2b):
                    e.wait_ge(s_go.h, v_x2b)

                @on("vector")
                def _(e, v_x2b=v_x2b):
                    e.wait_ge(s_go.h, v_x2b)

                @on("tensor")
                def _(e, v_x2b=v_x2b):
                    e.wait_ge(s_go.h, v_x2b)

                emit_ln(x2f_g, x2b_g, 128, 0, ad_cb1, ad_sb, 0, None, hp_g, h2_d, rt * 128, True)

        if stage >= 7:
            h2_dv = h2_d.rearrange("(b p) t -> b p t", p=128)
            h_dv = h_d.rearrange("(b p) t -> b p t", p=128)
            wguv = wgu_d.ap()
            wprv = wpr_d.ap()
            out_v = out_fm.rearrange("(b p) t -> b p t", p=128)

            @on("sync")
            def _(e):
                e.wait_ge(s_c.h, v_wq_gathered)
                s_x.dma(e.dma_start(out=wprsnap_d.ap(), in_=wpr_d.ap()))
                s_x.dma(e.dma_start(out=wgusnap_d.ap(), in_=wgu_d.ap()))

            @on("scalar")
            def _(e):
                e.wait_ge(s_c.h, v_wq_gathered)

            for th in range(2):
                @on("scalar")
                def _(e, th=th):
                    e.wait_ge(s_t.h, s_t.v)
                    e.wait_ge(s_x.h, s_x.v)
                    for b in range(NB):
                        s_sq.dma(e.dma_start(out=h2_sb[:, b * 512:(b + 1) * 512],
                                             in_=h2_dv[b, :, th * 512:(th + 1) * 512]))
                v_h2 = s_sq.v

                guv = []
                for g in range(64):
                    bko = (g % 2) * 2
                    @on("sync")
                    def _(e, g=g, th=th):
                        if g >= 2:
                            e.wait_ge(s_t.h, guv[g - 2])
                        gp = gu_pan[g % 2]
                        for k in range(NB):
                            s_x.dma(e.dma_start(out=gp[:, k * 128:(k + 1) * 128],
                                                in_=wguv[k * 128:(k + 1) * 128, g * 128:(g + 1) * 128]))
                        gp2 = gu_pan2[g % 2]
                        for k in range(NB):
                            s_x.dma(e.dma_start(out=gp2[:, k * 128:(k + 1) * 128],
                                                in_=wguv[k * 128:(k + 1) * 128, (64 + g) * 128:(65 + g) * 128]))
                    v_gp = s_x.v

                    @on("tensor")
                    def _(e, g=g, bko=bko, v_gp=v_gp, v_h2=v_h2):
                        e.wait_ge(s_x.h, v_gp)
                        e.wait_ge(s_sq.h, v_h2)
                        e.wait_ge(s_v.h, s_v.v)
                        e.wait_ge(s_s.h, s_s.v)
                        for k in range(NB):
                            s_t.inc(e.matmul(pb(bko), gu_pan[g % 2][:, k * 128:(k + 1) * 128],
                                             h2_sb[:, k * 512:(k + 1) * 512], start=(k == 0), stop=(k == NB - 1)))
                        for k in range(NB):
                            s_t.inc(e.matmul(pb(bko + 1), gu_pan2[g % 2][:, k * 128:(k + 1) * 128],
                                             h2_sb[:, k * 512:(k + 1) * 512], start=(k == 0), stop=(k == NB - 1)))
                    guv.append(s_t.v)
                    v_gu = s_t.v

                    @on("scalar")
                    def _(e, bko=bko, v_gu=v_gu):
                        e.wait_ge(s_t.h, v_gu)
                        e.wait_ge(s_v.h, s_v.v)
                        s_s.inc(e.activation(silu_t[:], pb(bko), AF.Silu))
                    v_si = s_s.v

                    @on("vector")
                    def _(e, bko=bko, v_si=v_si):
                        e.wait_ge(s_s.h, s_s.v)
                        e.wait_ge(s_x.h, s_x.v)
                        s_v.inc(e.tensor_mul(h_t[:], silu_t[:], pb(bko + 1)))
                        e.wait_ge(s_v.h, s_v.v)
                    v_ht = s_v.v

                    @on("sync")
                    def _(e, g=g, th=th, v_ht=v_ht):
                        e.wait_ge(s_v.h, v_ht)
                        s_x.dma(e.dma_start(out=h_dv[g, :, th * 512:(th + 1) * 512], in_=h_t[:]))

                    @on("scalar")
                    def _(e, v_ht=v_ht):
                        e.wait_ge(s_v.h, v_ht)
                        e.wait_ge(s_t.h, s_t.v)
                        s_s.inc(e.square(hsq_t[:], h_t[:]))
                    v_hs = s_s.v

                    @on("tensor")
                    def _(e, g=g, v_hs=v_hs):
                        e.wait_ge(s_s.h, v_hs)
                        for qr in range(4):
                            s_t.inc(e.matmul(pb(4 + qr, 0, 1), hsq_t[:, qr * 128:(qr + 1) * 128],
                                             k_o128b[:, :], start=(g == 0), stop=(g == 63)))
                v_sqmm = s_t.v

                @on("vector")
                def _(e, v_sqmm=v_sqmm):
                    e.wait_ge(s_t.h, v_sqmm)
                    for qr in range(4):
                        s_v.inc(e.tensor_copy(ffs[:, qr:qr + 1], pb(4 + qr, 0, 1)))
                    e.wait_ge(s_v.h, s_v.v)
                    s_v.inc(e.tensor_scalar(ffs[:], ffs[:], 1.0 / HIDP, None, op0=ALU.mult, op1=ALU.bypass))
                    e.wait_ge(s_v.h, s_v.v)
                v_ffs = s_v.v

                @on("scalar")
                def _(e, v_ffs=v_ffs):
                    e.wait_ge(s_v.h, v_ffs)
                    s_s.inc(e.activation(ffs[:], ffs[:], AF.Ln, bias=k_eps[:, 2:3]))
                    e.wait_ge(s_s.h, s_s.v)
                    s_s.inc(e.activation(ffs[:], ffs[:], AF.Exp, scale=-0.5))
                v_ffr = s_s.v

                @on("tensor")
                def _(e, v_ffr=v_ffr):
                    e.wait_ge(s_s.h, v_ffr)
                    for qr in range(4):
                        s_t.inc(e.transpose(pb1(6, qr * 128, 128), ffs[:, qr:qr + 1], k_identf[:]))
                v_ffT = s_t.v

                @on("vector")
                def _(e, v_ffT=v_ffT):
                    e.wait_ge(s_t.h, v_ffT)
                    s_v.inc(e.tensor_copy(ffrow[:], pb1(6, 0, 512)))
                    e.wait_ge(s_v.h, s_v.v)
                v_ffrow = s_v.v

                @on("tensor")
                def _(e, v_ffrow=v_ffrow):
                    e.wait_ge(s_v.h, v_ffrow)
                    s_t.inc(e.matmul(pb(7), k_o1x128r[:, :], ffrow[:, 0:512], start=True, stop=True))
                v_fbc = s_t.v

                @on("vector")
                def _(e, v_fbc=v_fbc):
                    e.wait_ge(s_t.h, v_fbc)
                    s_v.inc(e.tensor_copy(rstdffn_sb[:], pb(7, 0, 512)))
                    e.wait_ge(s_v.h, s_v.v)
                v_rsb = s_v.v

                for _eng in ("sync", "scalar", "tensor", "vector", "gpsimd"):
                    @on(_eng)
                    def _(e):
                        e.wait_ge(s_x.h, s_x.v)
                        e.wait_ge(s_sq.h, s_sq.v)
                        e.wait_ge(s_g.h, s_g.v)
                        e.wait_ge(s_go.h, s_go.v)
                        e.wait_ge(s_t.h, s_t.v)
                        e.wait_ge(s_v.h, s_v.v)
                        e.wait_ge(s_s.h, s_s.v)
                        e.wait_ge(s_c.h, s_c.v)

                for dh in range(2):
                    v_t0f = s_t.v
                    for hb2 in range(64):
                        @on("scalar")
                        def _(e, hb2=hb2, dh=dh, th=th, v_t0f=v_t0f):
                            if hb2 >= 2:
                                e.wait_ge(s_t.h, v_t0f + 8 * (hb2 - 1))
                            else:
                                e.wait_ge(s_t.h, v_t0f)
                                e.wait_ge(s_x.h, s_x.v)
                            s_sq.dma(e.dma_start(out=pr_pan[hb2 % 2][:],
                                                 in_=wprv[hb2 * 128:(hb2 + 1) * 128, dh * 1024:(dh + 1) * 1024]))
                            s_sq.dma(e.dma_start(out=hproj_t[hb2 % 2][:],
                                                 in_=h_dv[hb2, :, th * 512:(th + 1) * 512]))
                        v_pp = s_sq.v

                        @on("tensor")
                        def _(e, hb2=hb2, v_pp=v_pp, v_rsb=v_rsb):
                            e.wait_ge(s_sq.h, v_pp)
                            if hb2 == 0:
                                e.wait_ge(s_v.h, s_v.v)
                            for db in range(8):
                                s_t.inc(e.matmul(pb(db), pr_pan[hb2 % 2][:, db * 128:(db + 1) * 128],
                                                 hproj_t[hb2 % 2][:], start=(hb2 == 0), stop=(hb2 == 63)))
                    v_pmm = s_t.v

                    @on("vector")
                    def _(e, v_pmm=v_pmm):
                        e.wait_ge(s_t.h, v_pmm)
                        for _pad in range(3):
                            s_v.inc(e.tensor_copy(u1_g[:, 0:2048], x_g[:, 0:2048]))
                            e.wait_ge(s_v.h, s_v.v)

                    for db in range(8):
                        dbg_ = dh * 8 + db

                        @on("sync")
                        def _(e, dbg_=dbg_, th=th):
                            e.wait_ge(s_v.h, s_v.v)
                            e.wait_ge(s_x.h, s_x.v)
                            s_x.dma(e.dma_start(out=x2_t[:], in_=x2_dv[dbg_, :, th * 512:(th + 1) * 512]))
                        v_x2l = s_x.v

                        @on("vector")
                        def _(e, db=db, dbg_=dbg_, v_pmm=v_pmm, v_x2l=v_x2l):
                            e.wait_ge(s_t.h, v_pmm)
                            e.wait_ge(s_x.h, v_x2l)
                            s_v.inc(e.scalar_tensor_tensor(out_t[:], pb(db), ad_gb[:, dbg_:dbg_ + 1],
                                                           rstdffn_sb[:, 0:512], op0=ALU.mult, op1=ALU.mult))
                            e.wait_ge(s_v.h, s_v.v)
                            s_v.inc(e.tensor_add(out_t[:], out_t[:], x2_t[:]))
                            e.wait_ge(s_v.h, s_v.v)
                        v_ot = s_v.v

                        @on("sync")
                        def _(e, dbg_=dbg_, th=th, v_ot=v_ot):
                            e.wait_ge(s_v.h, v_ot)
                            s_x.dma(e.dma_start(out=out_v[dbg_, :, th * 512:(th + 1) * 512], in_=out_t[:]))

        @on("sync")
        def _(e):
            e.wait_ge(s_x.h, s_x.v)
            e.wait_ge(s_go.h, s_go.v)
            e.wait_ge(s_sq.h, s_sq.v)
            if stage >= 2:
                e.wait_ge(s_c.h, s_c.v)

        adend_d = dr("adend_d", [128, 192])

        @on("sync")
        def _(e):
            e.wait_ge(s_v.h, s_v.v)
            s_x.dma(e.dma_start(out=adend_d[:, 0:16], in_=ad_sa[:]))
            s_x.dma(e.dma_start(out=adend_d[:, 16:32], in_=ad_ca1[:]))
            s_x.dma(e.dma_start(out=adend_d[:, 32:48], in_=ad_ga[:]))
            s_x.dma(e.dma_start(out=adend_d[:, 48:64], in_=ad_sb[:]))
            s_x.dma(e.dma_start(out=adend_d[:, 64:80], in_=ad_cb1[:]))
            s_x.dma(e.dma_start(out=adend_d[:, 80:96], in_=ad_gb[:]))
            s_x.dma(e.dma_start(out=adend_d[:, 96:192], in_=mod_fm[:]))

        dbgsrc = {"mod_d": (mod_d, [NCORE, MODSH], F32), "hp_d": (hp_d, [D, TT], BF16),
                  "adend_d": (adend_d, [128, 192], F32), "adsnap_d": (adsnap_d, [128, 64], F32),
                  "wprsnap_d": (wprsnap_d, [HIDP, D], BF16), "wgusnap_d": (wgusnap_d, [D, HID], BF16),
                  "wpr_sh_d2": (wpr_sh_d, [PRSH, D], BF16),
                  "wgu_sh_d": (wgu_sh_d, [GUSH, HID], BF16), "wpr_sh_d": (wpr_sh_d, [PRSH, D], BF16),
                  "wgu_d": (wgu_d, [D, HID], BF16), "q_d": (q_d, [TT, D], BF16),
                  "sigf_d": (sigf_d, [TT, D], F32), "v_d": (v_d, [TT, D], BF16),
                  "o_d": (o_d, [D, T], F32), "x2_d": (x2_d, [D, T], F32),
                  "h2_d": (h2_d, [D, T], BF16), "h_d": (h_d, [HIDP, T], BF16),
                  "dbgln_d": (dbgln_d, [128, 512], F32)}
        for name in dbg:
            src, shp, dt = dbgsrc[name]
            dbg_t[name] = nc.dram_tensor("dbg_" + name, shp, dt, kind="ExternalOutput").ap()

        @on("scalar")
        def _(e):
            e.wait_ge(s_x.h, s_x.v)
            e.wait_ge(s_go.h, s_go.v)
            e.wait_ge(s_v.h, s_v.v)
            e.wait_ge(s_s.h, s_s.v)
            for name in dbg:
                src, shp, dt = dbgsrc[name]
                s_sq.dma(e.dma_start(out=dbg_t[name][:, :], in_=src.ap()))
            e.wait_ge(s_sq.h, s_sq.v)

    st.close()
    return nc, dbg_t


def _host_prepare(inputs):
    import ml_dtypes
    bf = ml_dtypes.bfloat16
    x = np.ascontiguousarray(np.asarray(inputs["L0_attn_x"]).reshape(8192, D))
    xTf = np.ascontiguousarray(x.T)
    WqT = np.ascontiguousarray(np.asarray(inputs["L0_attn_Wq"]).T)
    WkT = np.ascontiguousarray(np.asarray(inputs["L0_attn_Wk"]).T)
    WvT = np.ascontiguousarray(np.asarray(inputs["L0_attn_Wv"]).T)
    WoT = np.ascontiguousarray(np.asarray(inputs["L0_attn_Wo"]).T)
    wguT = np.ascontiguousarray(np.asarray(inputs["L0_ffn_w_gate_up"]).T)
    wprT = np.ascontiguousarray(np.asarray(inputs["L0_ffn_w_proj"]).T)
    wmod = np.asarray(inputs["L0_wmod"])
    ccb = np.ascontiguousarray(np.asarray(inputs["L0_c"]).reshape(D, 1))
    nwc = np.ascontiguousarray(np.asarray(inputs["L0_attn_norm_w"]).reshape(D, 1))

    U = np.triu(np.ones((64, 64), np.float32))
    u64d = np.zeros((128, 128), np.float32)
    u64d[:64, :64] = U
    u64d[64:, 64:] = U
    e2f = np.zeros((2, 128), np.float32)
    e2f[0, :64] = 1
    e2f[1, 64:] = 1
    e2c = (e2f * np.float32(C_THR)).astype(bf)
    e2b = e2f.astype(bf)
    g64f = np.zeros((128, 2), np.float32)
    g64f[:64, 0] = 1
    g64f[64:, 1] = 1
    mask1 = -np.triu(np.ones((64, 64), np.float32))
    masktn = np.zeros((128, 512), np.float32)
    masktn[:64] = np.tile(mask1, (1, 8))
    masktn[64:] = np.tile(mask1, (1, 8))

    in_maps = []
    for r in range(NCORE):
        xsl = np.zeros((D, TT), np.float32)
        lo = 1024 * r
        xsl[:, TW:] = xTf[:, lo:lo + T]
        wm = np.ones((128, 1), np.float32)
        if r > 0:
            xsl[:, :TW] = xTf[:, lo - TW:lo]
        else:
            wm[:] = 0.0
        in_maps.append({
            "xT": np.ascontiguousarray(xsl),
            "WqT": WqT, "WkT": WkT, "WvT": WvT, "WoT": WoT,
            "nwc": nwc, "cc": ccb,
            "wmod_sl": np.ascontiguousarray(wmod[:, MODSH * r:MODSH * (r + 1)]),
            "wguT_sl": np.ascontiguousarray(wguT[GUSH * r:GUSH * (r + 1)]),
            "wprT_sl": np.ascontiguousarray(wprT[PRSH * r:PRSH * (r + 1)]),
            "wmask": wm,
            "u64d": u64d, "e2c": e2c, "e2b": e2b, "g64f": g64f, "g64b": g64f.astype(bf),
            "ones128b": np.ones((128, 1), bf), "ones1x128": np.ones((1, 128), np.float32),
            "identf": np.eye(128, dtype=np.float32), "identb": np.eye(128, dtype=np.float32).astype(bf),
            "masktn": masktn.astype(bf),
            "epsc": np.tile(np.array([[LN_EPS, RMS_EPS, FFN_EPS, 0.0]], np.float32), (128, 1)),
        })
    return in_maps


def kernel(**inputs):
    in_maps = _host_prepare(inputs)
    nc, _ = build(DBG, STAGE)
    res = run_bass_kernel_spmd(nc, in_maps, core_ids=list(range(NCORE)))
    outs = [r["out_fm"] for r in res.results]
    full = np.concatenate([o.T for o in outs], axis=0)
    return full.reshape(1, 8192, D).astype(np.float32)


# version 14
# speedup vs baseline: 1.0451x; 1.0451x over previous
import contextlib
import numpy as np
import concourse.bass as bass
import concourse.mybir as mybir
from concourse.bass_utils import run_bass_kernel_spmd

F32 = mybir.dt.float32
F32R = mybir.dt.float32r
BF16 = mybir.dt.bfloat16
AF = mybir.ActivationFunctionType
ALU = mybir.AluOpType
I16 = mybir.dt.int16
I32 = mybir.dt.int32

NCORE = 8
D = 2048
NB = 16
T = 1024
TW = 128
TT = T + TW
NTILE = TT // 128
H = 16
MODSH = 12288 // NCORE
HID = 16384
HIDP = 8192
GUSH = D // NCORE
PRSH = HIDP // NCORE
C_THR = 0.5009765625
LN_EPS = 1e-5
RMS_EPS = 1e-6
FFN_EPS = float(np.finfo(np.float32).eps)

DBG = ()
STAGE = 99
QSUB = 99
DBGLN = False


class Sem:
    __slots__ = ("h", "v")

    def __init__(self, st, nc, name):
        self.h = st.enter_context(nc.semaphore(name))
        self.v = 0

    def inc(self, instr, n=1):
        instr.then_inc(self.h, n)
        self.v += n
        return instr

    def dma(self, instr):
        return self.inc(instr, 16)

    def cc(self, instr):
        instr.then_inc(self.h)
        self.v += 1
        return instr


def build(dbg=(), stage=99):
    nc = bass.Bass("TRN2", target_bir_lowering=False)
    st = contextlib.ExitStack()
    nc._keep = st

    def inp(name, shape, dt=F32):
        return nc.dram_tensor(name, shape, dt, kind="ExternalInput").ap()

    xT = inp("xT", [D, TT])
    WqT = inp("WqT", [D, D])
    WkT = inp("WkT", [D, D])
    WvT = inp("WvT", [D, D])
    WoT = inp("WoT", [D, D])
    nwc = inp("nwc", [D, 1])
    wmod_sl = inp("wmod_sl", [D, MODSH])
    cc = inp("cc", [D, 1])
    wguT_sl = inp("wguT_sl", [GUSH, HID])
    wprT_sl = inp("wprT_sl", [PRSH, D])
    wmask = inp("wmask", [128, 1])
    u64d = inp("u64d", [128, 128])
    e2c = inp("e2c", [2, 128], BF16)
    e2b = inp("e2b", [2, 128], BF16)
    g64f = inp("g64f", [128, 2])
    g64b = inp("g64b", [128, 2], BF16)
    ones128b = inp("ones128b", [128, 1], BF16)
    ones1x128 = inp("ones1x128", [1, 128])
    identf = inp("identf", [128, 128])
    identb = inp("identb", [128, 128], BF16)
    masktn = inp("masktn", [128, 512], BF16)
    epsc = inp("epsc", [128, 4])

    out_fm = nc.dram_tensor("out_fm", [D, T], F32, kind="ExternalOutput").ap()

    dr = lambda n, s, dt=F32: nc.dram_tensor(n, s, dt)
    adsnap_d = None
    adsnap_d = dr("adsnap_d", [128, 64])
    wprsnap_d = dr("wprsnap_d", [HIDP, D], BF16)
    wgusnap_d = dr("wgusnap_d", [D, HID], BF16)
    modsh_d = dr("modsh_d", [1, MODSH])
    mod_d = dr("mod_d", [NCORE, MODSH])
    hp_d = dr("hp_d", [D, TT], BF16)
    q_d = dr("q_d", [TT, D], BF16)
    sigf_d = dr("sigf_d", [TT, D], F32)
    v_d = dr("v_d", [TT, D], BF16)
    o_d = dr("o_d", [D, T], F32)
    x2_d = dr("x2_d", [D, T], F32)
    h2_d = dr("h2_d", [D, T], BF16)
    qdT_d = dr("qdT_d", [128, 256], BF16)
    wgu_sh_d = dr("wgu_sh_d", [GUSH, HID], BF16)
    wpr_sh_d = dr("wpr_sh_d", [PRSH, D], BF16)
    wgu_d = dr("wgu_d", [D, HID], BF16)
    ccdum_d = dr("ccdum_d", [1, 128])
    ccdum2_d = dr("ccdum2_d", [NCORE, 128])
    wpr_d = dr("wpr_d", [HIDP, D], BF16)
    h_d = dr("h_d", [HIDP, T], BF16)
    dbgln_d = dr("dbgln_d", [128, 512], F32)

    sb = lambda n, s, dt: st.enter_context(nc.sbuf_tensor(n, s, dt))
    sem = lambda n: Sem(st, nc, n)

    arena = sb("arena", [128, 4096], F32)
    _b = nc.sbuf_base if not callable(nc.sbuf_base) else nc.sbuf_base()
    ar0 = _b - 16384
    wpanr = [nc.alloc_sbuf_tensor_at(f"wpanr{i}", [128, MODSH], F32R, offset=ar0 + i * 6144)
             for i in range(2)]
    f_t = nc.alloc_sbuf_tensor_at("f_t", [128, 2048], F32, offset=ar0)
    h2_sb = nc.alloc_sbuf_tensor_at("h2_sb", [128, NB * 512], BF16, offset=ar0)

    k_u64r = sb("k_u64r", [128, 128], F32R)
    k_g64r = sb("k_g64r", [128, 2], F32R)
    k_e2c = sb("k_e2c", [2, 128], BF16)
    k_e2b = sb("k_e2b", [2, 128], BF16)
    k_g64b = sb("k_g64b", [128, 2], BF16)
    k_o128b = sb("k_o128b", [128, 1], BF16)
    k_o1x128r = sb("k_o1x128r", [1, 128], F32R)
    k_identf = sb("k_identf", [128, 128], F32)
    k_identb = sb("k_identb", [128, 128], BF16)
    k_masktn = sb("k_masktn", [128, 512], BF16)
    k_wmask = sb("k_wmask", [128, 1], F32)
    k_eps = sb("k_eps", [128, 4], F32)
    k_nw = sb("k_nw", [128, 16], F32)
    k_c = sb("k_c", [128, 16], F32)
    silu_c = sb("silu_c", [128, 16], F32R)
    rows16 = sb("rows16", [16, 128 * 6], F32)
    mod_fm = sb("mod_fm", [128, 96], F32)
    ad_sa = sb("ad_sa", [128, 16], F32)
    ad_ca1 = sb("ad_ca1", [128, 16], F32)
    ad_ga = sb("ad_ga", [128, 16], F32)
    ad_sb = sb("ad_sb", [128, 16], F32)
    ad_cb1 = sb("ad_cb1", [128, 16], F32)
    ad_gb = sb("ad_gb", [128, 16], F32)
    sgn_s = sb("sgn_s", [128, 32], F32)
    mod_part = sb("mod_part", [1, MODSH], F32)

    qw_bf = [sb(f"qw_bf{i}", [128, 512], BF16) for i in range(2)]
    qwa = sb("qwa", [128, 512], BF16)
    qdense = sb("qdense", [128, 256], F32)
    qdbf = sb("qdbf", [128, 256], BF16)
    qdT = sb("qdT", [128, 256], BF16)
    qspair = sb("qspair", [2, 512], BF16)
    qsgn = sb("qsgn", [128, 512], BF16)
    qp = sb("qp", [128, 512], BF16)
    qhat = sb("qhat", [128, 512], BF16)
    qout = sb("qout", [128, 512], BF16)

    x_g = sb("x_g", [128, NB * 128], F32)
    xbf_g = sb("xbf_g", [128, NB * 128], BF16)
    xsq_g = sb("xsq_g", [128, NB * 128], BF16)
    u1_g = sb("u1_g", [128, NB * 128], F32)
    hp_g = sb("hp_g", [128, NB * 128], BF16)
    stat2 = sb("stat2", [128, 2], F32)
    mean_t = sb("mean_t", [128, 1], F32)
    var_t = sb("var_t", [128, 1], F32)
    var2_t = sb("var2_t", [128, 1], F32)
    rstd_t = sb("rstd_t", [128, 1], F32)
    nmr_t = sb("nmr_t", [128, 1], F32)
    row2 = sb("row2", [1, 256], F32R)
    hp_t = [sb(f"hp_t{i}", [128, NB * 128], BF16) for i in range(2)]
    wproj_pan = [sb(f"wproj_pan{i}", [128, 1024], BF16) for i in range(2)]
    wproj_pan2 = [sb(f"wproj_pan2{i}", [128, 1024], BF16) for i in range(2)]
    sf_ev = sb("sf_ev", [128, 2048], F32)
    q_p = sb("q_p", [128, 2048], BF16)
    v_p = sb("v_p", [128, 2048], BF16)
    sigf_p = sb("sigf_p", [128, 2048], F32)
    logf_p = sb("logf_p", [128, 2048], F32R)
    expA_p = sb("expA_p", [128, 2048], BF16)
    expnA_p = sb("expnA_p", [128, 2048], BF16)
    qdec_p = sb("qdec_p", [128, 2048], BF16)
    kdec_p = sb("kdec_p", [128, 2048], BF16)
    qdecT_p = sb("qdecT_p", [128, 2048], BF16)
    kdecT_p = sb("kdecT_p", [128, 2048], BF16)
    scoresT_p = sb("scoresT_p", [128, 1024], BF16)
    o_t = sb("o_t", [128, 1024], F32)
    d_all = sb("d_all", [128, NTILE * 32], F32)
    dsum_t = sb("dsum_t", [128, 32], F32)
    T_st = sb("T_st", [128, H * 128], F32)
    S_bf = sb("S_bf", [128, H * 128], BF16)
    onT_t = sb("onT_t", [128, NB * 128], BF16)
    wo_pan = [sb(f"wo_pan{i}", [128, 2048], BF16) for i in range(2)]
    gu_pan = [sb(f"gu_pan{i}", [128, 2048], BF16) for i in range(2)]
    gu_pan2 = [sb(f"gu_pan2{i}", [128, 2048], BF16) for i in range(2)]
    silu_t = sb("silu_t", [128, 512], BF16)
    h_t = sb("h_t", [128, 512], BF16)
    hsq_t = sb("hsq_t", [128, 512], BF16)
    ffs = sb("ffs", [128, 4], F32)
    ffrow = sb("ffrow", [1, 512], F32R)
    rstdffn_sb = sb("rstdffn_sb", [128, 512], F32)
    pr_pan = [sb(f"pr_pan{i}", [128, 1024], BF16) for i in range(2)]
    hproj_t = [sb(f"hproj_t{i}", [128, 512], BF16) for i in range(2)]
    x2_t = sb("x2_t", [128, 512], F32)
    out_t = sb("out_t", [128, 512], F32)

    pbig = st.enter_context(nc.psum_tensor("pbig", [128, 4096], F32))

    def pb(bank, off=0, w=512):
        return pbig[:, bank * 512 + off: bank * 512 + off + w]

    def pb1(bank, off, w):
        return pbig[0:1, bank * 512 + off: bank * 512 + off + w]

    pbig_bf = pbig.bitcast(BF16)

    def pbb(bank, off=0, w=1024):
        return pbig_bf[:, bank * 1024 + off: bank * 1024 + off + w]

    s_x = sem("s_x")
    s_g = sem("s_g")
    s_sq = sem("s_sq")
    s_t = sem("s_t")
    s_v = sem("s_v")
    s_s = sem("s_s")
    s_c = sem("s_c")
    s_go = sem("s_go")

    dbg_t = {}

    with nc.Block() as blk:
        def on(eng):
            return getattr(blk, eng)

        @on("sync")
        def _(e):
            s_x.dma(e.dma_start(out=k_e2c[:], in_=e2c[:, :]))
            s_x.dma(e.dma_start(out=k_e2b[:], in_=e2b[:, :]))
            s_x.dma(e.dma_start(out=k_g64b[:], in_=g64b[:, :]))
            s_x.dma(e.dma_start(out=k_o128b[:], in_=ones128b[:, :]))
            s_x.dma(e.dma_start(out=k_identf[:], in_=identf[:, :]))
            s_x.dma(e.dma_start(out=k_identb[:], in_=identb[:, :]))
            s_x.dma(e.dma_start(out=k_masktn[:], in_=masktn[:, :]))
            s_x.dma(e.dma_start(out=k_wmask[:], in_=wmask[:, :]))
            s_x.dma(e.dma_start(out=k_eps[:], in_=epsc[:, :]))
        v_kx = s_x.v

        @on("gpsimd")
        def _(e):
            s_g.dma(e.dma_start(out=k_u64r[:], in_=u64d[:, :]))
            s_g.dma(e.dma_start(out=k_g64r[:], in_=g64f[:, :]))
            s_g.dma(e.dma_start(out=k_o1x128r[:], in_=ones1x128[:, :]))
        v_kg = s_g.v

        @on("sync")
        def _(e):
            s_x.dma(e.dma_start(out=rows16[:, 0:128], in_=cc.rearrange("(a p) o -> a (p o)", a=16)[:, :]))
            s_x.dma(e.dma_start(out=rows16[:, 128:256], in_=nwc.rearrange("(a p) o -> a (p o)", a=16)[:, :]))
        v_cnw = s_x.v

        @on("tensor")
        def _(e):
            e.wait_ge(s_x.h, v_cnw)
            s_t.inc(e.transpose(pbig[0:128, 0:16], rows16[:, 0:128], k_identf[0:16, 0:16]))
            s_t.inc(e.transpose(pbig[0:128, 16:32], rows16[:, 128:256], k_identf[0:16, 0:16]))
        v_cnwT = s_t.v

        @on("vector")
        def _(e):
            e.wait_ge(s_t.h, v_cnwT)
            s_v.inc(e.tensor_copy(k_c[:], pbig[:, 0:16]))
            s_v.inc(e.tensor_copy(k_nw[:], pbig[:, 16:32]))
            e.wait_ge(s_v.h, s_v.v)
        v_cnwE = s_v.v

        @on("scalar")
        def _(e):
            e.wait_ge(s_v.h, v_cnwE)
            s_s.inc(e.activation(silu_c[:], k_c[:], AF.Silu))
        v_siluc = s_s.v

        wv_mod = wmod_sl.rearrange("(b p) m -> b p m", p=128)
        v_t_mod0 = s_t.v

        @on("gpsimd")
        def _(e):
            for k in range(NB):
                if k >= 2:
                    e.wait_ge(s_t.h, v_t_mod0 + 3 * (k - 1))
                s_g.dma(e.dma_start(out=wpanr[k % 2][:], in_=wv_mod[k, :, :]))
        v_modpan0 = v_kg + 16

        @on("tensor")
        def _(e):
            e.wait_ge(s_s.h, v_siluc)
            for k in range(NB):
                e.wait_ge(s_g.h, v_modpan0 + 16 * k)
                for j in range(3):
                    s_t.inc(e.matmul(pb1(j, 0, 512), silu_c[:, k:k + 1], wpanr[k % 2][:, j * 512:(j + 1) * 512],
                                     start=(k == 0), stop=(k == NB - 1)))
        v_t_mod = s_t.v

        @on("vector")
        def _(e):
            e.wait_ge(s_t.h, v_t_mod)
            for j in range(3):
                s_v.inc(e.tensor_copy(mod_part[:, j * 512:(j + 1) * 512], pb1(j, 0, 512)))
        v_v_mod = s_v.v

        @on("scalar")
        def _(e):
            e.wait_ge(s_v.h, v_v_mod)
            s_sq.dma(e.dma_start(out=modsh_d[:, :], in_=mod_part[:]))
        v_modst = s_sq.v

        @on("gpsimd")
        def _(e):
            e.wait_ge(s_sq.h, v_modst)
            s_c.cc(e.collective_compute(
                "AllGather", ALU.bypass, replica_groups=[list(range(NCORE))],
                ins=[modsh_d.ap().opt()], outs=[mod_d.ap().opt()]))
            e.wait_ge(s_c.h, s_c.v)
            mfl = mod_d.ap().rearrange("r m -> (r m)")
            mv16 = mfl.rearrange("(s a p) -> s a p", a=16, p=128)
            for sv in range(6):
                s_g.dma(e.dma_start(out=rows16[:, sv * 128:(sv + 1) * 128], in_=mv16[sv, :, :]))
        v_modfm = s_g.v

        @on("tensor")
        def _(e):
            e.wait_ge(s_g.h, v_modfm)
            for sv in range(6):
                s_t.inc(e.transpose(pbig[0:128, 64 + sv * 16: 64 + (sv + 1) * 16], rows16[:, sv * 128:(sv + 1) * 128],
                                    k_identf[0:16, 0:16]))
        v_modT = s_t.v

        @on("vector")
        def _(e):
            e.wait_ge(s_t.h, v_modT)
            s_v.inc(e.tensor_copy(mod_fm[:], pbig[:, 64:160]))
            e.wait_ge(s_v.h, s_v.v)
        v_modfm2 = s_v.v

        @on("scalar")
        def _(e):
            e.wait_ge(s_v.h, v_modfm2)
            s_s.inc(e.sign(sgn_s[:, 0:16], mod_fm[:, 32:48]))
            s_s.inc(e.sign(sgn_s[:, 16:32], mod_fm[:, 80:96]))
        v_sgnm = s_s.v

        @on("vector")
        def _(e):
            s_v.inc(e.tensor_copy(ad_sa[:], mod_fm[:, 0:16]))
            s_v.inc(e.tensor_scalar(ad_ca1[:], mod_fm[:, 16:32], 1.0, None, op0=ALU.add, op1=ALU.bypass))
            s_v.inc(e.tensor_copy(ad_sb[:], mod_fm[:, 48:64]))
            s_v.inc(e.tensor_scalar(ad_cb1[:], mod_fm[:, 64:80], 1.0, None, op0=ALU.add, op1=ALU.bypass))
            e.wait_ge(s_s.h, v_sgnm)
            s_v.inc(e.tensor_scalar(ad_ga[:].bitcast(I32), mod_fm[:, 32:48].bitcast(I32), 0x7FFFFFFF, None,
                                    op0=ALU.bitwise_and, op1=ALU.bypass))
            e.wait_ge(s_v.h, s_v.v)
            s_v.inc(e.tensor_scalar(ad_ga[:], ad_ga[:], 0.25, None, op0=ALU.max, op1=ALU.bypass))
            e.wait_ge(s_v.h, s_v.v)
            s_v.inc(e.tensor_mul(ad_ga[:], ad_ga[:], sgn_s[:, 0:16]))
            s_v.inc(e.tensor_scalar(ad_gb[:].bitcast(I32), mod_fm[:, 80:96].bitcast(I32), 0x7FFFFFFF, None,
                                    op0=ALU.bitwise_and, op1=ALU.bypass))
            e.wait_ge(s_v.h, s_v.v)
            s_v.inc(e.tensor_scalar(ad_gb[:], ad_gb[:], 0.25, None, op0=ALU.max, op1=ALU.bypass))
            e.wait_ge(s_v.h, s_v.v)
            s_v.inc(e.tensor_mul(ad_gb[:], ad_gb[:], sgn_s[:, 16:32]))
            e.wait_ge(s_v.h, s_v.v)
        v_adaln = s_v.v

        if stage >= 2:
            qblocks = [("gu", b, wguT_sl, wgu_sh_d, HID) for b in range(GUSH // 128)] + \
                      [("pr", b, wprT_sl, wpr_sh_d, D) for b in range(PRSH // 128)]
            for (tag, qb, src, dst, width) in qblocks:
                ntl = width // 512
                srcv = src.rearrange("(a p) m -> a p m", p=128)
                dstv = dst.rearrange("(a p) m -> a p m", p=128)

                v_t0 = s_t.v
                v_v0 = s_v.v
                for j in range(ntl):
                    @on("gpsimd")
                    def _(e, j=j):
                        e.wait_ge(s_v.h, s_v.v)
                        e.wait_ge(s_t.h, s_t.v)
                        s_g.dma(e.dma_start(out=qw_bf[j % 2][:], in_=srcv[qb, :, j * 512:(j + 1) * 512]))
                    v_w = s_g.v

                    @on("vector")
                    def _(e, j=j, v_w=v_w):
                        e.wait_ge(s_g.h, v_w)
                        e.wait_ge(s_t.h, s_t.v)
                        s_v.inc(e.tensor_scalar(qwa[:].bitcast(I16), qw_bf[j % 2][:].bitcast(I16), 0x7FFF, None,
                                                op0=ALU.bitwise_and, op1=ALU.bypass))
                        e.wait_ge(s_v.h, s_v.v)
                    v_wa = s_v.v

                    @on("tensor")
                    def _(e, j=j, v_wa=v_wa):
                        e.wait_ge(s_v.h, v_wa)
                        for qr in range(4):
                            sl = qwa[:, qr * 128:(qr + 1) * 128]
                            s_t.inc(e.matmul(pb(6, 4 * j + qr, 1), sl, k_g64b[:, 0:1], start=True, stop=True))
                            s_t.inc(e.matmul(pb(6, 128 + 4 * j + qr, 1), sl, k_g64b[:, 1:2], start=True, stop=True))
                v_sums = s_t.v

                @on("vector")
                def _(e):
                    e.wait_ge(s_t.h, v_sums)
                    s_v.inc(e.tensor_scalar(qdense[:], pb(6, 0, 256), 1.0 / 64.0, None, op0=ALU.mult, op1=ALU.bypass))
                    e.wait_ge(s_v.h, s_v.v)
                    s_v.inc(e.tensor_copy(qdbf[:], qdense[:]))
                    e.wait_ge(s_v.h, s_v.v)
                    s_v.inc(e.tensor_scalar(qdbf[:], qdbf[:], 1e-8, None, op0=ALU.max, op1=ALU.bypass))
                    e.wait_ge(s_v.h, s_v.v)
                v_dense = s_v.v

                @on("tensor")
                def _(e):
                    e.wait_ge(s_v.h, v_dense)
                    s_t.inc(e.transpose(pbb(7, 0, 256)[:, 0:128], qdbf[:, 0:128], k_identb[:]))
                    s_t.inc(e.transpose(pbb(7, 0, 256)[:, 128:256], qdbf[:, 128:256], k_identb[:]))
                v_dT = s_t.v

                @on("vector")
                def _(e):
                    e.wait_ge(s_t.h, v_dT)
                    s_v.inc(e.tensor_copy(qdT[:], pbb(7, 0, 256)))
                v_dTe = s_v.v

                @on("scalar")
                def _(e):
                    e.wait_ge(s_v.h, v_dTe)
                    s_sq.dma(e.dma_start(out=qdT_d[:, :], in_=qdT[:, :]))
                v_dTd = s_sq.v

                for j in range(ntl if QSUB >= 2 else 0):
                    @on("scalar")
                    def _(e, j=j, v_dTd=v_dTd):
                        e.wait_ge(s_sq.h, v_dTd)
                        e.wait_ge(s_t.h, s_t.v)
                        e.wait_ge(s_v.h, s_v.v)
                        flat = qdT_d.ap().rearrange("r m -> (r m)")
                        src_ap = flat[(4 * j) * 256:(4 * j + 4) * 256].rearrange(
                            "(q h m) -> h q m", q=4, h=2)
                        s_sq.dma(e.dma_start(out=qspair[:, :].rearrange("h (q m) -> h q m", q=4), in_=src_ap))
                    v_sp = s_sq.v

                    @on("tensor")
                    def _(e, v_sp=v_sp):
                        e.wait_ge(s_sq.h, v_sp)
                        s_t.inc(e.matmul(pb(7), k_e2c[:, :], qspair[:, :], start=True, stop=True))
                        s_t.inc(e.matmul(pb(5), k_e2b[:, :], qspair[:, :], start=True, stop=True))
                    v_bc = s_t.v

                    if QSUB < 3:
                        continue
                    @on("gpsimd")
                    def _(e, j=j):
                        e.wait_ge(s_v.h, s_v.v)
                        s_g.dma(e.dma_start(out=qw_bf[j % 2][:], in_=srcv[qb, :, j * 512:(j + 1) * 512]))
                    v_w2 = s_g.v

                    @on("scalar")
                    def _(e, j=j, v_w2=v_w2):
                        e.wait_ge(s_g.h, v_w2)
                        e.wait_ge(s_v.h, s_v.v)
                        s_s.inc(e.sign(qsgn[:], qw_bf[j % 2][:]))
                    v_sgn2 = s_s.v

                    @on("vector")
                    def _(e, j=j, v_w2=v_w2, v_bc=v_bc, v_sgn2=v_sgn2):
                        e.wait_ge(s_g.h, v_w2)
                        s_v.inc(e.tensor_scalar(qwa[:].bitcast(I16), qw_bf[j % 2][:].bitcast(I16), 0x7FFF, None,
                                                op0=ALU.bitwise_and, op1=ALU.bypass))
                        if QSUB >= 4:
                            e.wait_ge(s_t.h, v_bc)
                            s_v.inc(e.tensor_tensor(qp[:], qwa[:], pb(7), op=ALU.is_gt))
                            e.wait_ge(s_s.h, v_sgn2)
                            s_v.inc(e.tensor_mul(qp[:], qp[:], qsgn[:]))
                        if QSUB >= 5:
                            s_v.inc(e.scalar_tensor_tensor(qhat[:], qp[:], 1.0, pb(5), op0=ALU.mult, op1=ALU.mult))
                        if QSUB >= 5:
                            s_v.inc(e.tensor_sub(qout[:], qhat[:], qw_bf[j % 2][:]))
                            s_v.inc(e.tensor_add(qout[:], qw_bf[j % 2][:], qout[:]))
                    v_q = s_v.v

                    if QSUB >= 6:
                        @on("gpsimd")
                        def _(e, j=j, v_q=v_q):
                            e.wait_ge(s_v.h, v_q)
                            s_go.dma(e.dma_start(out=dstv[qb, :, j * 512:(j + 1) * 512], in_=qout[:]))

            @on("gpsimd")
            def _(e):
                e.wait_ge(s_go.h, s_go.v)
                if QSUB < 7:
                    return
                s_c.cc(e.collective_compute(
                    "AllGather", ALU.bypass, replica_groups=[list(range(NCORE))],
                    ins=[wgu_sh_d.ap().opt()], outs=[wgu_d.ap().opt()]))
                s_c.cc(e.collective_compute(
                    "AllGather", ALU.bypass, replica_groups=[list(range(NCORE))],
                    ins=[wpr_sh_d.ap().opt()], outs=[wpr_d.ap().opt()]))
                s_c.cc(e.collective_compute(
                    "AllGather", ALU.bypass, replica_groups=[list(range(NCORE))],
                    ins=[ccdum_d.ap().opt()], outs=[ccdum2_d.ap().opt()]))
            v_wq_gathered = s_c.v

        @on("sync")
        def _(e):
            e.wait_ge(s_v.h, s_v.v)
            s_x.dma(e.dma_start(out=adsnap_d[0:128, 0:16], in_=ad_cb1[:]))

        xTv = xT.rearrange("(b p) t -> b p t", p=128)

        def emit_ln(src_fm_f32, src_fm_bf, stride, coloff, ca1, sa_, eps_col, gammav,
                    dst_sb, dst_dram, dst_col, sub_mean):
            @on("tensor")
            def _(e):
                e.wait_ge(s_v.h, s_v.v)

            @on("scalar")
            def _(e):
                e.wait_ge(s_t.h, s_t.v)
                for b in range(NB):
                    s_s.inc(e.square(xsq_g[:, b * 128:(b + 1) * 128],
                                     src_fm_bf[:, b * stride + coloff: b * stride + coloff + 128]))
            v_sq = s_s.v

            @on("tensor")
            def _(e):
                if sub_mean:
                    for b in range(NB):
                        s_t.inc(e.matmul(pb(6, 0, 1), src_fm_bf[:, b * stride + coloff: b * stride + coloff + 128],
                                         k_o128b[:, :], start=(b == 0), stop=(b == NB - 1)))
                e.wait_ge(s_s.h, v_sq)
                for b in range(NB):
                    s_t.inc(e.matmul(pb(6, 1, 1), xsq_g[:, b * 128:(b + 1) * 128],
                                     k_o128b[:, :], start=(b == 0), stop=(b == NB - 1)))
            v_st = s_t.v

            @on("vector")
            def _(e):
                e.wait_ge(s_t.h, v_st)
                s_v.inc(e.tensor_copy(stat2[:], pb(6, 0, 2)))
                e.wait_ge(s_v.h, s_v.v)
                if sub_mean:
                    s_v.inc(e.tensor_scalar(mean_t[:], stat2[:, 0:1], 1.0 / D, None, op0=ALU.mult, op1=ALU.bypass))
                    s_v.inc(e.tensor_scalar(var_t[:], stat2[:, 1:2], 1.0 / D, None, op0=ALU.mult, op1=ALU.bypass))
                    e.wait_ge(s_v.h, s_v.v)
                    s_v.inc(e.scalar_tensor_tensor(var2_t[:], mean_t[:], mean_t[:, 0:1], var_t[:],
                                                   op0=ALU.mult, op1=ALU.subtract))
                else:
                    s_v.inc(e.tensor_scalar(var2_t[:], stat2[:, 1:2], -1.0 / D, None, op0=ALU.mult, op1=ALU.bypass))
                e.wait_ge(s_v.h, s_v.v)
            v_vv = s_v.v

            @on("scalar")
            def _(e):
                e.wait_ge(s_v.h, v_vv)
                s_s.inc(e.activation(nmr_t[:], var2_t[:], AF.Ln, bias=k_eps[:, eps_col:eps_col + 1], scale=-1.0))
                e.wait_ge(s_s.h, s_s.v)
                s_s.inc(e.activation(rstd_t[:], nmr_t[:], AF.Exp, scale=-0.5))
            v_rs = s_s.v

            @on("vector")
            def _(e):
                e.wait_ge(s_s.h, v_rs)
                if sub_mean:
                    s_v.inc(e.scalar_tensor_tensor(nmr_t[:], mean_t[:], -1.0, rstd_t[:], op0=ALU.mult, op1=ALU.mult))
                else:
                    s_v.inc(e.memset(nmr_t[:], 0.0))
                e.wait_ge(s_v.h, s_v.v)
            v_rr = s_v.v

            @on("tensor")
            def _(e):
                e.wait_ge(s_v.h, v_rr)
                s_t.inc(e.transpose(pb1(7, 0, 128), rstd_t[:], k_identf[:]))
                s_t.inc(e.transpose(pb1(7, 128, 128), nmr_t[:], k_identf[:]))
            v_tr = s_t.v

            @on("vector")
            def _(e):
                e.wait_ge(s_t.h, v_tr)
                s_v.inc(e.tensor_copy(row2[:], pb1(7, 0, 256)))
            v_r2 = s_v.v

            @on("tensor")
            def _(e):
                e.wait_ge(s_v.h, v_r2)
                s_t.inc(e.matmul(pb(6, 0, 128), k_o1x128r[:, :], row2[:, 0:128], start=True, stop=True))
                s_t.inc(e.matmul(pb(7, 0, 128), k_o1x128r[:, :], row2[:, 128:256], start=True, stop=True))
            v_bc2 = s_t.v

            if DBGLN and dst_col == 0 and dst_dram is not None:
                @on("vector")
                def _(e):
                    e.wait_ge(s_t.h, v_bc2)
                    s_v.inc(e.tensor_copy(u1_g[:, 1920:2048], pb(6, 0, 128)))
                    s_v.inc(e.tensor_copy(u1_g[:, 1792:1920], pb(7, 0, 128)))
                    s_v.inc(e.tensor_copy(u1_g[:, 1780:1782], stat2[:, 0:2]))
                    s_v.inc(e.tensor_copy(u1_g[:, 1782:1783], mean_t[:]))
                    s_v.inc(e.tensor_copy(u1_g[:, 1783:1784], var2_t[:]))
                    s_v.inc(e.tensor_copy(u1_g[:, 1784:1785], rstd_t[:]))
                    s_v.inc(e.tensor_copy(u1_g[:, 1785:1786], nmr_t[:]))
                vdbg = s_v.v

                @on("sync")
                def _(e):
                    e.wait_ge(s_v.h, vdbg)
                    s_x.dma(e.dma_start(out=dbgln_d[:, 0:268], in_=u1_g[:, 1780:2048]))

            @on("vector")
            def _(e):
                e.wait_ge(s_t.h, v_bc2)
                for b in range(NB):
                    so = b * stride + coloff
                    s_v.inc(e.tensor_mul(u1_g[:, b * 128:(b + 1) * 128],
                                         src_fm_f32[:, so:so + 128], pb(6, 0, 128)))
                    e.wait_ge(s_v.h, s_v.v)
                    s_v.inc(e.tensor_add(u1_g[:, b * 128:(b + 1) * 128],
                                         u1_g[:, b * 128:(b + 1) * 128], pb(7, 0, 128)))
                    e.wait_ge(s_v.h, s_v.v)
                    if gammav is not None and ca1 is None:
                        s_v.inc(e.tensor_scalar(dst_sb[:, b * 128:(b + 1) * 128], u1_g[:, b * 128:(b + 1) * 128],
                                                gammav[:, b:b + 1], None, op0=ALU.mult, op1=ALU.bypass))
                        continue
                    if gammav is not None:
                        s_v.inc(e.tensor_scalar(u1_g[:, b * 128:(b + 1) * 128], u1_g[:, b * 128:(b + 1) * 128],
                                                gammav[:, b:b + 1], None, op0=ALU.mult, op1=ALU.bypass))
                        e.wait_ge(s_v.h, s_v.v)
                    s_v.inc(e.tensor_scalar(dst_sb[:, b * 128:(b + 1) * 128], u1_g[:, b * 128:(b + 1) * 128],
                                            ca1[:, b:b + 1], sa_[:, b:b + 1], op0=ALU.mult, op1=ALU.add))
            v_aff = s_v.v

            if dst_dram is not None:
                dv = dst_dram.rearrange("(b p) t -> b p t", p=128)

                @on("sync")
                def _(e):
                    e.wait_ge(s_v.h, v_aff)
                    for b in range(NB):
                        s_x.dma(e.dma_start(out=dv[b, :, dst_col:dst_col + 128],
                                            in_=dst_sb[:, b * 128:(b + 1) * 128]))
            return v_aff

        if stage >= 3:
            for tt in range(NTILE):
                @on("sync")
                def _(e, tt=tt):
                    e.wait_ge(s_v.h, s_v.v)
                    for b in range(NB):
                        s_x.dma(e.dma_start(out=x_g[:, b * 128:(b + 1) * 128],
                                            in_=xTv[b, :, tt * 128:(tt + 1) * 128]))
                v_xl = s_x.v

                @on("gpsimd")
                def _(e, tt=tt):
                    e.wait_ge(s_t.h, s_t.v)
                    for b in range(NB):
                        s_g.dma(e.dma_start(out=xbf_g[:, b * 128:(b + 1) * 128],
                                            in_=xTv[b, :, tt * 128:(tt + 1) * 128]))
                v_xbl = s_g.v

                @on("scalar")
                def _(e, v_xbl=v_xbl):
                    e.wait_ge(s_g.h, v_xbl)

                @on("vector")
                def _(e, v_xl=v_xl, v_xbl=v_xbl):
                    e.wait_ge(s_x.h, v_xl)
                    e.wait_ge(s_g.h, v_xbl)

                @on("tensor")
                def _(e, v_xbl=v_xbl):
                    e.wait_ge(s_g.h, v_xbl)
                emit_ln(x_g, xbf_g, 128, 0, ad_ca1, ad_sa, 0, None, hp_g, hp_d, tt * 128, True)

        @on("sync")
        def _(e):
            e.wait_ge(s_v.h, s_v.v)
            e.wait_ge(s_s.h, s_s.v)
            s_x.dma(e.dma_start(out=adsnap_d[0:128, 16:32], in_=ad_cb1[:]))

        if stage >= 4:
            hp_dv = hp_d.rearrange("(b p) t -> b p t", p=128)
            prj_ev = hp_g

            for tt in range(NTILE):
                @on("scalar")
                def _(e, tt=tt):
                    e.wait_ge(s_t.h, s_t.v)
                    e.wait_ge(s_x.h, s_x.v)
                    for b in range(NB):
                        s_sq.dma(e.dma_start(out=hp_t[tt % 2][:, b * 128:(b + 1) * 128],
                                             in_=hp_dv[b, :, tt * 128:(tt + 1) * 128]))
                v_hp = s_sq.v

                for ip, (Wt, kind) in enumerate(((WqT, "q"), (WkT, "k"), (WvT, "v"))):
                    Wv_ = Wt.rearrange("(b p) m -> b p m", p=128)
                    v_t0p = s_t.v
                    v_pan = []
                    for k in range(NB):
                        @on("gpsimd")
                        def _(e, k=k, v_t0p=v_t0p):
                            if k >= 2:
                                e.wait_ge(s_t.h, v_t0p + 4 * (k - 1))
                            else:
                                e.wait_ge(s_t.h, v_t0p)
                            s_g.dma(e.dma_start(out=wproj_pan[k % 2][:], in_=Wv_[k, :, 0:1024]))
                            s_g.dma(e.dma_start(out=wproj_pan2[k % 2][:], in_=Wv_[k, :, 1024:2048]))
                        v_pan.append(s_g.v)

                    @on("tensor")
                    def _(e, v_hp=v_hp, v_pan=v_pan, tt=tt):
                        e.wait_ge(s_sq.h, v_hp)
                        e.wait_ge(s_v.h, s_v.v)
                        e.wait_ge(s_s.h, s_s.v)
                        for k in range(NB):
                            e.wait_ge(s_g.h, v_pan[k])
                            lt = hp_t[tt % 2][:, k * 128:(k + 1) * 128]
                            st0, sp0 = (k == 0), (k == NB - 1)
                            e_ = e.matmul(pb(0), lt, wproj_pan[k % 2][:, 0:512], start=st0, stop=sp0)
                            s_t.inc(e_)
                            s_t.inc(e.matmul(pb(1), lt, wproj_pan[k % 2][:, 512:1024], start=st0, stop=sp0))
                            s_t.inc(e.matmul(pb(2), lt, wproj_pan2[k % 2][:, 0:512], start=st0, stop=sp0))
                            s_t.inc(e.matmul(pb(3), lt, wproj_pan2[k % 2][:, 512:1024], start=st0, stop=sp0))
                    v_mm = s_t.v

                    if kind == "q":
                        @on("scalar")
                        def _(e, v_mm=v_mm):
                            e.wait_ge(s_t.h, v_mm)
                            e.wait_ge(s_x.h, s_x.v)
                            for n in range(4):
                                s_s.inc(e.activation(prj_ev[:, n * 512:(n + 1) * 512], pb(n), AF.Silu))
                        v_ev = s_s.v

                        @on("sync")
                        def _(e, v_ev=v_ev, tt=tt):
                            e.wait_ge(s_s.h, v_ev)
                            s_x.dma(e.dma_start(out=q_d[tt * 128:(tt + 1) * 128, :], in_=prj_ev[:]))
                    elif kind == "k":
                        @on("scalar")
                        def _(e, v_mm=v_mm):
                            e.wait_ge(s_t.h, v_mm)
                            e.wait_ge(s_x.h, s_x.v)
                            for n in range(4):
                                s_s.inc(e.activation(f_t[:, n * 512:(n + 1) * 512], pb(n), AF.Sigmoid))
                            e.wait_ge(s_s.h, s_s.v)
                            for n in range(4):
                                s_s.inc(e.activation(sf_ev[:, n * 512:(n + 1) * 512],
                                                     f_t[:, n * 512:(n + 1) * 512], AF.Sigmoid))
                        v_ev = s_s.v

                        @on("sync")
                        def _(e, v_ev=v_ev, tt=tt):
                            e.wait_ge(s_s.h, v_ev)
                            s_x.dma(e.dma_start(out=sigf_d[tt * 128:(tt + 1) * 128, :], in_=sf_ev[:]))
                    else:
                        @on("vector")
                        def _(e, v_mm=v_mm, tt=tt):
                            e.wait_ge(s_t.h, v_mm)
                            e.wait_ge(s_x.h, s_x.v)
                            for n in range(4):
                                if tt == 0:
                                    s_v.inc(e.tensor_scalar(prj_ev[:, n * 512:(n + 1) * 512], pb(n),
                                                            k_wmask[:, 0:1], None, op0=ALU.mult, op1=ALU.bypass))
                                else:
                                    s_v.inc(e.tensor_copy(prj_ev[:, n * 512:(n + 1) * 512], pb(n)))
                        v_ev = s_v.v

                        @on("sync")
                        def _(e, v_ev=v_ev, tt=tt):
                            e.wait_ge(s_v.h, v_ev)
                            s_x.dma(e.dma_start(out=v_d[tt * 128:(tt + 1) * 128, :], in_=prj_ev[:]))

        @on("sync")
        def _(e):
            e.wait_ge(s_v.h, s_v.v)
            e.wait_ge(s_s.h, s_s.v)
            s_x.dma(e.dma_start(out=adsnap_d[0:128, 32:48], in_=ad_cb1[:]))

        if stage >= 5:
            @on("gpsimd")
            def _(e):
                e.memset(T_st[:], 0.0)
                s_g.inc(e.nop())
            v_Tz = s_g.v

            @on("vector")
            def _(e):
                e.wait_ge(s_g.h, v_Tz)

            o_dv = o_d.rearrange("(b p) t -> b p t", p=128)
            for p_ in range(NTILE):
                @on("scalar")
                def _(e, p_=p_):
                    e.wait_ge(s_x.h, s_x.v)
                    e.wait_ge(s_t.h, s_t.v)
                    e.wait_ge(s_v.h, s_v.v)
                    s_sq.dma(e.dma_start(out=q_p[:], in_=q_d[p_ * 128:(p_ + 1) * 128, :]))
                    s_sq.dma(e.dma_start(out=v_p[:], in_=v_d[p_ * 128:(p_ + 1) * 128, :]))
                    s_sq.dma(e.dma_start(out=sigf_p[:], in_=sigf_d[p_ * 128:(p_ + 1) * 128, :]))
                v_ld = s_sq.v

                @on("scalar")
                def _(e, v_ld=v_ld):
                    e.wait_ge(s_sq.h, v_ld)
                    s_s.inc(e.activation(logf_p[:], sigf_p[:], AF.Ln))
                v_lf = s_s.v

                @on("tensor")
                def _(e, v_lf=v_lf):
                    e.wait_ge(s_s.h, v_lf)
                    e.wait_ge(s_v.h, s_v.v)
                    for j in range(4):
                        s_t.inc(e.matmul(pb(j), k_u64r[:, :], logf_p[:, j * 512:(j + 1) * 512],
                                         start=True, stop=True))
                    for b in range(NB):
                        s_t.inc(e.matmul(pb(4, 2 * b, 2), logf_p[:, b * 128:(b + 1) * 128],
                                         k_g64r[:, :], start=True, stop=True))
                v_Amm = s_t.v

                @on("scalar")
                def _(e, v_Amm=v_Amm):
                    e.wait_ge(s_t.h, v_Amm)
                    for j in range(4):
                        s_s.inc(e.activation(expA_p[:, j * 512:(j + 1) * 512], pb(j), AF.Exp))
                    for j in range(4):
                        s_s.inc(e.activation(expnA_p[:, j * 512:(j + 1) * 512], pb(j), AF.Exp, scale=-1.0))
                v_exps = s_s.v

                @on("vector")
                def _(e, v_Amm=v_Amm, p_=p_):
                    e.wait_ge(s_t.h, v_Amm)
                    s_v.inc(e.tensor_copy(dsum_t[:], pb(4, 0, 32)))
                v_ds = s_v.v

                @on("scalar")
                def _(e, v_ds=v_ds, p_=p_):
                    e.wait_ge(s_v.h, v_ds)
                    s_s.inc(e.activation(d_all[:, p_ * 32:(p_ + 1) * 32], dsum_t[:], AF.Exp))
                v_dex = s_s.v

                @on("vector")
                def _(e, v_exps=v_exps, v_ld=v_ld):
                    e.wait_ge(s_s.h, v_exps)
                    s_v.inc(e.tensor_mul(qdec_p[:], q_p[:], expA_p[:]))
                    s_v.inc(e.scalar_tensor_tensor(kdec_p[:], sigf_p[:], 1.0, expnA_p[:],
                                                   op0=ALU.subtract, op1=ALU.mult))
                v_dec = s_v.v

                for (srcT, dstT) in ((qdec_p, qdecT_p), (kdec_p, kdecT_p)):
                    for half in range(2):
                        @on("tensor")
                        def _(e, srcT=srcT, half=half, v_dec=v_dec):
                            e.wait_ge(s_v.h, v_dec)
                            e.wait_ge(s_v.h, s_v.v)
                            for i in range(8):
                                b = half * 8 + i
                                s_t.inc(e.transpose(pbb(5, i * 128, 128), srcT[:, b * 128:(b + 1) * 128],
                                                    k_identb[:]))
                        v_tf = s_t.v

                        @on("vector")
                        def _(e, dstT=dstT, half=half, v_tf=v_tf):
                            e.wait_ge(s_t.h, v_tf)
                            s_v.inc(e.tensor_copy(dstT[:, half * 1024:(half + 1) * 1024], pbb(5, 0, 1024)))
                v_trdone = s_v.v

                for c_ in range(2):
                    cg = 2 * p_ + c_
                    par = 64 * c_
                    real = (p_ > 0)
                    @on("vector")
                    def _(e, cg=cg, v_dex=v_dex):
                        e.wait_ge(s_v.h, s_v.v)
                        if cg > 0:
                            e.wait_ge(s_s.h, s_s.v)
                        for h in range(H):
                            if cg == 0:
                                s_v.inc(e.tensor_scalar(S_bf[:, h * 128:(h + 1) * 128],
                                                        T_st[:, h * 128:(h + 1) * 128],
                                                        1.0, None, op0=ALU.mult, op1=ALU.bypass))
                            else:
                                pp0, pr0 = (cg - 1) // 2, (cg - 1) % 2
                                s_v.inc(e.tensor_scalar(S_bf[:, h * 128:(h + 1) * 128],
                                                        T_st[:, h * 128:(h + 1) * 128],
                                                        d_all[:, pp0 * 32 + 2 * h + pr0: pp0 * 32 + 2 * h + pr0 + 1],
                                                        None, op0=ALU.mult, op1=ALU.bypass))
                        e.wait_ge(s_v.h, s_v.v)
                    v_sbf = s_v.v

                    if real:
                        @on("tensor")
                        def _(e, par=par, v_trdone=v_trdone):
                            e.wait_ge(s_v.h, v_trdone)
                            for h in range(H):
                                bank = 6 + (h // 8)
                                s_t.inc(e.matmul(pbig[par:par + 64, bank * 512 + 64 * (h % 8): bank * 512 + 64 * (h % 8) + 64],
                                                 kdecT_p[:, h * 128 + par: h * 128 + par + 64],
                                                 qdecT_p[:, h * 128 + par: h * 128 + par + 64],
                                                 start=True, stop=True))
                        v_sc = s_t.v

                        @on("vector")
                        def _(e, par=par, v_sc=v_sc):
                            e.wait_ge(s_t.h, v_sc)
                            for n in range(2):
                                s_v.inc(e.tensor_mul(scoresT_p[par:par + 64, n * 512:(n + 1) * 512],
                                                     pbig[par:par + 64, (6 + n) * 512:(7 + n) * 512],
                                                     k_masktn[par:par + 64, :]))
                        v_msk = s_v.v

                        @on("tensor")
                        def _(e, par=par, v_msk=v_msk, v_sbf=v_sbf):
                            e.wait_ge(s_v.h, v_msk)
                            for h in range(H):
                                bank = h // 8
                                osl = pbig[:, bank * 512 + 64 * (h % 8): bank * 512 + 64 * (h % 8) + 64]
                                s_t.inc(e.matmul(osl, v_p[par:par + 64, h * 128:(h + 1) * 128],
                                                 scoresT_p[par:par + 64, 64 * h: 64 * h + 64],
                                                 start=True, stop=False))
                                s_t.inc(e.matmul(osl, S_bf[:, h * 128:(h + 1) * 128],
                                                 qdecT_p[:, h * 128 + par: h * 128 + par + 64],
                                                 start=False, stop=True))
                        v_omm = s_t.v

                    @on("tensor")
                    def _(e, par=par, v_trdone=v_trdone, v_sbf=v_sbf):
                        e.wait_ge(s_v.h, v_sbf)
                        for h in range(H):
                            bank = 2 + h // 4
                            s_t.inc(e.matmul(pbig[:, bank * 512 + 128 * (h % 4): bank * 512 + 128 * (h % 4) + 128],
                                             kdec_p[par:par + 64, h * 128:(h + 1) * 128],
                                             v_p[par:par + 64, h * 128:(h + 1) * 128],
                                             start=True, stop=True))
                    v_smm = s_t.v

                    @on("vector")
                    def _(e, cg=cg, v_smm=v_smm):
                        e.wait_ge(s_t.h, v_smm)
                        for h in range(H):
                            bank = 2 + h // 4
                            msl = pbig[:, bank * 512 + 128 * (h % 4): bank * 512 + 128 * (h % 4) + 128]
                            if cg == 0:
                                s_v.inc(e.scalar_tensor_tensor(T_st[:, h * 128:(h + 1) * 128],
                                                               T_st[:, h * 128:(h + 1) * 128], 1.0, msl,
                                                               op0=ALU.mult, op1=ALU.subtract))
                            else:
                                pp0, pr0 = (cg - 1) // 2, (cg - 1) % 2
                                s_v.inc(e.scalar_tensor_tensor(T_st[:, h * 128:(h + 1) * 128],
                                                               T_st[:, h * 128:(h + 1) * 128],
                                                               d_all[:, pp0 * 32 + 2 * h + pr0: pp0 * 32 + 2 * h + pr0 + 1],
                                                               msl, op0=ALU.mult, op1=ALU.subtract))
                        e.wait_ge(s_v.h, s_v.v)
                    v_tupd = s_v.v

                    if real:
                        @on("vector")
                        def _(e, v_omm=v_omm):
                            e.wait_ge(s_t.h, v_omm)
                            e.wait_ge(s_x.h, s_x.v)
                            s_v.inc(e.tensor_copy(o_t[:, 0:512], pb(0)))
                            s_v.inc(e.tensor_copy(o_t[:, 512:1024], pb(1)))
                        v_oev = s_v.v

                        @on("sync")
                        def _(e, p_=p_, c_=c_, v_oev=v_oev):
                            e.wait_ge(s_v.h, v_oev)
                            tok0 = 128 * p_ + 64 * c_ - TW
                            for h in range(H):
                                s_x.dma(e.dma_start(out=o_dv[h, :, tok0:tok0 + 64],
                                                    in_=o_t[:, 64 * h:64 * h + 64]))

        @on("sync")
        def _(e):
            e.wait_ge(s_v.h, s_v.v)
            e.wait_ge(s_s.h, s_s.v)
            s_x.dma(e.dma_start(out=adsnap_d[0:128, 48:64], in_=ad_cb1[:]))

        if stage >= 6:
            o_dv6 = o_d.rearrange("(b p) t -> b p t", p=128)
            x2_dv = x2_d.rearrange("(b p) t -> b p t", p=128)
            WoTv = WoT.rearrange("(b p) m -> b p m", p=128)
            x2f_g = sigf_p
            x2b_g = q_p

            for rt in range(8):
                @on("sync")
                def _(e, rt=rt):
                    e.wait_ge(s_v.h, s_v.v)
                    e.wait_ge(s_x.h, s_x.v)
                    for b in range(NB):
                        s_x.dma(e.dma_start(out=x_g[:, b * 128:(b + 1) * 128],
                                            in_=o_dv6[b, :, rt * 128:(rt + 1) * 128]))
                v_ol = s_x.v

                @on("gpsimd")
                def _(e, rt=rt):
                    e.wait_ge(s_t.h, s_t.v)
                    e.wait_ge(s_x.h, s_x.v)
                    for b in range(NB):
                        s_g.dma(e.dma_start(out=xbf_g[:, b * 128:(b + 1) * 128],
                                            in_=o_dv6[b, :, rt * 128:(rt + 1) * 128]))
                v_obl = s_g.v

                @on("scalar")
                def _(e, v_obl=v_obl):
                    e.wait_ge(s_g.h, v_obl)

                @on("vector")
                def _(e, v_ol=v_ol):
                    e.wait_ge(s_x.h, v_ol)

                emit_ln(x_g, xbf_g, 128, 0, None, None, 1, k_nw, onT_t, None, 0, False)
                v_on = s_v.v

                v_t0w = s_t.v
                for f in range(NB):
                    @on("gpsimd")
                    def _(e, f=f, v_t0w=v_t0w):
                        if f >= 2:
                            e.wait_ge(s_t.h, v_t0w + 16 * (f - 1))
                        else:
                            e.wait_ge(s_t.h, v_t0w)
                        s_g.dma(e.dma_start(out=wo_pan[f % 2][:], in_=WoTv[f, :, :]))
                    v_wp = s_g.v

                    @on("tensor")
                    def _(e, f=f, v_wp=v_wp, v_on=v_on):
                        e.wait_ge(s_g.h, v_wp)
                        if f == 0:
                            e.wait_ge(s_v.h, s_v.v)
                        for db in range(NB):
                            s_t.inc(e.matmul(pbig[:, db * 128:(db + 1) * 128],
                                             wo_pan[f % 2][:, db * 128:(db + 1) * 128],
                                             onT_t[:, f * 128:(f + 1) * 128],
                                             start=(f == 0 and db % 4 == 0), stop=(f == NB - 1),
                                             skip_group_check=True))
                v_wmm = s_t.v

                @on("sync")
                def _(e, rt=rt):
                    e.wait_ge(s_v.h, s_v.v)
                    for b in range(NB):
                        s_x.dma(e.dma_start(out=x_g[:, b * 128:(b + 1) * 128],
                                            in_=xTv[b, :, TW + rt * 128: TW + rt * 128 + 128]))
                v_xr = s_x.v

                @on("vector")
                def _(e, v_wmm=v_wmm, v_xr=v_xr):
                    e.wait_ge(s_t.h, v_wmm)
                    e.wait_ge(s_x.h, v_xr)
                    e.wait_ge(s_go.h, s_go.v)
                    for db in range(NB):
                        s_v.inc(e.scalar_tensor_tensor(x2f_g[:, db * 128:(db + 1) * 128],
                                                       pbig[:, db * 128:(db + 1) * 128],
                                                       ad_ga[:, db:db + 1], x_g[:, db * 128:(db + 1) * 128],
                                                       op0=ALU.mult, op1=ALU.add))
                    e.wait_ge(s_v.h, s_v.v)
                v_x2 = s_v.v

                @on("sync")
                def _(e, rt=rt, v_x2=v_x2):
                    e.wait_ge(s_v.h, v_x2)
                    for b in range(NB):
                        s_x.dma(e.dma_start(out=x2_dv[b, :, rt * 128:(rt + 1) * 128],
                                            in_=x2f_g[:, b * 128:(b + 1) * 128]))

                @on("gpsimd")
                def _(e, v_x2=v_x2):
                    e.wait_ge(s_v.h, v_x2)
                    s_go.dma(e.dma_start(out=x2b_g[:], in_=x2f_g[:]))
                v_x2b = s_go.v

                @on("scalar")
                def _(e, v_x2b=v_x2b):
                    e.wait_ge(s_go.h, v_x2b)

                @on("vector")
                def _(e, v_x2b=v_x2b):
                    e.wait_ge(s_go.h, v_x2b)

                @on("tensor")
                def _(e, v_x2b=v_x2b):
                    e.wait_ge(s_go.h, v_x2b)

                emit_ln(x2f_g, x2b_g, 128, 0, ad_cb1, ad_sb, 0, None, hp_g, h2_d, rt * 128, True)

        if stage >= 7:
            h2_dv = h2_d.rearrange("(b p) t -> b p t", p=128)
            h_dv = h_d.rearrange("(b p) t -> b p t", p=128)
            wguv = wgu_d.ap()
            wprv = wpr_d.ap()
            out_v = out_fm.rearrange("(b p) t -> b p t", p=128)

            @on("sync")
            def _(e):
                e.wait_ge(s_c.h, v_wq_gathered)

            @on("scalar")
            def _(e):
                e.wait_ge(s_c.h, v_wq_gathered)

            for th in range(2):
                @on("scalar")
                def _(e, th=th):
                    e.wait_ge(s_t.h, s_t.v)
                    e.wait_ge(s_x.h, s_x.v)
                    for b in range(NB):
                        s_sq.dma(e.dma_start(out=h2_sb[:, b * 512:(b + 1) * 512],
                                             in_=h2_dv[b, :, th * 512:(th + 1) * 512]))
                v_h2 = s_sq.v

                guv = []
                for g in range(64):
                    bko = (g % 2) * 2
                    @on("sync")
                    def _(e, g=g, th=th):
                        if g >= 2:
                            e.wait_ge(s_t.h, guv[g - 2])
                        gp = gu_pan[g % 2]
                        for k in range(NB):
                            s_x.dma(e.dma_start(out=gp[:, k * 128:(k + 1) * 128],
                                                in_=wguv[k * 128:(k + 1) * 128, g * 128:(g + 1) * 128]))
                        gp2 = gu_pan2[g % 2]
                        for k in range(NB):
                            s_x.dma(e.dma_start(out=gp2[:, k * 128:(k + 1) * 128],
                                                in_=wguv[k * 128:(k + 1) * 128, (64 + g) * 128:(65 + g) * 128]))
                    v_gp = s_x.v

                    @on("tensor")
                    def _(e, g=g, bko=bko, v_gp=v_gp, v_h2=v_h2):
                        e.wait_ge(s_x.h, v_gp)
                        e.wait_ge(s_sq.h, v_h2)
                        e.wait_ge(s_v.h, s_v.v)
                        e.wait_ge(s_s.h, s_s.v)
                        for k in range(NB):
                            s_t.inc(e.matmul(pb(bko), gu_pan[g % 2][:, k * 128:(k + 1) * 128],
                                             h2_sb[:, k * 512:(k + 1) * 512], start=(k == 0), stop=(k == NB - 1)))
                        for k in range(NB):
                            s_t.inc(e.matmul(pb(bko + 1), gu_pan2[g % 2][:, k * 128:(k + 1) * 128],
                                             h2_sb[:, k * 512:(k + 1) * 512], start=(k == 0), stop=(k == NB - 1)))
                    guv.append(s_t.v)
                    v_gu = s_t.v

                    @on("scalar")
                    def _(e, bko=bko, v_gu=v_gu):
                        e.wait_ge(s_t.h, v_gu)
                        e.wait_ge(s_v.h, s_v.v)
                        s_s.inc(e.activation(silu_t[:], pb(bko), AF.Silu))
                    v_si = s_s.v

                    @on("vector")
                    def _(e, bko=bko, v_si=v_si):
                        e.wait_ge(s_s.h, s_s.v)
                        e.wait_ge(s_x.h, s_x.v)
                        s_v.inc(e.tensor_mul(h_t[:], silu_t[:], pb(bko + 1)))
                        e.wait_ge(s_v.h, s_v.v)
                    v_ht = s_v.v

                    @on("sync")
                    def _(e, g=g, th=th, v_ht=v_ht):
                        e.wait_ge(s_v.h, v_ht)
                        s_x.dma(e.dma_start(out=h_dv[g, :, th * 512:(th + 1) * 512], in_=h_t[:]))

                    @on("scalar")
                    def _(e, v_ht=v_ht):
                        e.wait_ge(s_v.h, v_ht)
                        e.wait_ge(s_t.h, s_t.v)
                        s_s.inc(e.square(hsq_t[:], h_t[:]))
                    v_hs = s_s.v

                    @on("tensor")
                    def _(e, g=g, v_hs=v_hs):
                        e.wait_ge(s_s.h, v_hs)
                        for qr in range(4):
                            s_t.inc(e.matmul(pb(4 + qr, 0, 1), hsq_t[:, qr * 128:(qr + 1) * 128],
                                             k_o128b[:, :], start=(g == 0), stop=(g == 63)))
                v_sqmm = s_t.v

                @on("vector")
                def _(e, v_sqmm=v_sqmm):
                    e.wait_ge(s_t.h, v_sqmm)
                    for qr in range(4):
                        s_v.inc(e.tensor_copy(ffs[:, qr:qr + 1], pb(4 + qr, 0, 1)))
                    e.wait_ge(s_v.h, s_v.v)
                    s_v.inc(e.tensor_scalar(ffs[:], ffs[:], 1.0 / HIDP, None, op0=ALU.mult, op1=ALU.bypass))
                    e.wait_ge(s_v.h, s_v.v)
                v_ffs = s_v.v

                @on("scalar")
                def _(e, v_ffs=v_ffs):
                    e.wait_ge(s_v.h, v_ffs)
                    s_s.inc(e.activation(ffs[:], ffs[:], AF.Ln, bias=k_eps[:, 2:3]))
                    e.wait_ge(s_s.h, s_s.v)
                    s_s.inc(e.activation(ffs[:], ffs[:], AF.Exp, scale=-0.5))
                v_ffr = s_s.v

                @on("tensor")
                def _(e, v_ffr=v_ffr):
                    e.wait_ge(s_s.h, v_ffr)
                    for qr in range(4):
                        s_t.inc(e.transpose(pb1(6, qr * 128, 128), ffs[:, qr:qr + 1], k_identf[:]))
                v_ffT = s_t.v

                @on("vector")
                def _(e, v_ffT=v_ffT):
                    e.wait_ge(s_t.h, v_ffT)
                    s_v.inc(e.tensor_copy(ffrow[:], pb1(6, 0, 512)))
                    e.wait_ge(s_v.h, s_v.v)
                v_ffrow = s_v.v

                @on("tensor")
                def _(e, v_ffrow=v_ffrow):
                    e.wait_ge(s_v.h, v_ffrow)
                    s_t.inc(e.matmul(pb(7), k_o1x128r[:, :], ffrow[:, 0:512], start=True, stop=True))
                v_fbc = s_t.v

                @on("vector")
                def _(e, v_fbc=v_fbc):
                    e.wait_ge(s_t.h, v_fbc)
                    s_v.inc(e.tensor_copy(rstdffn_sb[:], pb(7, 0, 512)))
                    e.wait_ge(s_v.h, s_v.v)
                v_rsb = s_v.v

                for dh in range(2):
                    v_t0f = s_t.v
                    for hb2 in range(64):
                        @on("scalar")
                        def _(e, hb2=hb2, dh=dh, th=th, v_t0f=v_t0f):
                            if hb2 >= 2:
                                e.wait_ge(s_t.h, v_t0f + 8 * (hb2 - 1))
                            else:
                                e.wait_ge(s_t.h, v_t0f)
                                e.wait_ge(s_x.h, s_x.v)
                            s_sq.dma(e.dma_start(out=pr_pan[hb2 % 2][:],
                                                 in_=wprv[hb2 * 128:(hb2 + 1) * 128, dh * 1024:(dh + 1) * 1024]))
                            s_sq.dma(e.dma_start(out=hproj_t[hb2 % 2][:],
                                                 in_=h_dv[hb2, :, th * 512:(th + 1) * 512]))
                        v_pp = s_sq.v

                        @on("tensor")
                        def _(e, hb2=hb2, v_pp=v_pp, v_rsb=v_rsb):
                            e.wait_ge(s_sq.h, v_pp)
                            if hb2 == 0:
                                e.wait_ge(s_v.h, s_v.v)
                            for db in range(8):
                                s_t.inc(e.matmul(pb(db), pr_pan[hb2 % 2][:, db * 128:(db + 1) * 128],
                                                 hproj_t[hb2 % 2][:], start=(hb2 == 0), stop=(hb2 == 63)))
                    v_pmm = s_t.v

                    for db in range(8):
                        dbg_ = dh * 8 + db

                        @on("sync")
                        def _(e, dbg_=dbg_, th=th):
                            e.wait_ge(s_v.h, s_v.v)
                            e.wait_ge(s_x.h, s_x.v)
                            s_x.dma(e.dma_start(out=x2_t[:], in_=x2_dv[dbg_, :, th * 512:(th + 1) * 512]))
                        v_x2l = s_x.v

                        @on("vector")
                        def _(e, db=db, dbg_=dbg_, v_pmm=v_pmm, v_x2l=v_x2l):
                            e.wait_ge(s_t.h, v_pmm)
                            e.wait_ge(s_x.h, v_x2l)
                            s_v.inc(e.scalar_tensor_tensor(out_t[:], pb(db), ad_gb[:, dbg_:dbg_ + 1],
                                                           rstdffn_sb[:, 0:512], op0=ALU.mult, op1=ALU.mult))
                            e.wait_ge(s_v.h, s_v.v)
                            s_v.inc(e.tensor_add(out_t[:], out_t[:], x2_t[:]))
                            e.wait_ge(s_v.h, s_v.v)
                        v_ot = s_v.v

                        @on("sync")
                        def _(e, dbg_=dbg_, th=th, v_ot=v_ot):
                            e.wait_ge(s_v.h, v_ot)
                            s_x.dma(e.dma_start(out=out_v[dbg_, :, th * 512:(th + 1) * 512], in_=out_t[:]))

        @on("sync")
        def _(e):
            e.wait_ge(s_x.h, s_x.v)
            e.wait_ge(s_go.h, s_go.v)
            e.wait_ge(s_sq.h, s_sq.v)
            if stage >= 2:
                e.wait_ge(s_c.h, s_c.v)

        adend_d = dr("adend_d", [128, 192])

        @on("sync")
        def _(e):
            e.wait_ge(s_v.h, s_v.v)
            s_x.dma(e.dma_start(out=adend_d[:, 0:16], in_=ad_sa[:]))
            s_x.dma(e.dma_start(out=adend_d[:, 16:32], in_=ad_ca1[:]))
            s_x.dma(e.dma_start(out=adend_d[:, 32:48], in_=ad_ga[:]))
            s_x.dma(e.dma_start(out=adend_d[:, 48:64], in_=ad_sb[:]))
            s_x.dma(e.dma_start(out=adend_d[:, 64:80], in_=ad_cb1[:]))
            s_x.dma(e.dma_start(out=adend_d[:, 80:96], in_=ad_gb[:]))
            s_x.dma(e.dma_start(out=adend_d[:, 96:192], in_=mod_fm[:]))

        dbgsrc = {"mod_d": (mod_d, [NCORE, MODSH], F32), "hp_d": (hp_d, [D, TT], BF16),
                  "adend_d": (adend_d, [128, 192], F32), "adsnap_d": (adsnap_d, [128, 64], F32),
                  "wprsnap_d": (wprsnap_d, [HIDP, D], BF16), "wgusnap_d": (wgusnap_d, [D, HID], BF16),
                  "wpr_sh_d2": (wpr_sh_d, [PRSH, D], BF16),
                  "wgu_sh_d": (wgu_sh_d, [GUSH, HID], BF16), "wpr_sh_d": (wpr_sh_d, [PRSH, D], BF16),
                  "wgu_d": (wgu_d, [D, HID], BF16), "q_d": (q_d, [TT, D], BF16),
                  "sigf_d": (sigf_d, [TT, D], F32), "v_d": (v_d, [TT, D], BF16),
                  "o_d": (o_d, [D, T], F32), "x2_d": (x2_d, [D, T], F32),
                  "h2_d": (h2_d, [D, T], BF16), "h_d": (h_d, [HIDP, T], BF16),
                  "dbgln_d": (dbgln_d, [128, 512], F32)}
        for name in dbg:
            src, shp, dt = dbgsrc[name]
            dbg_t[name] = nc.dram_tensor("dbg_" + name, shp, dt, kind="ExternalOutput").ap()

        @on("scalar")
        def _(e):
            e.wait_ge(s_x.h, s_x.v)
            e.wait_ge(s_go.h, s_go.v)
            e.wait_ge(s_v.h, s_v.v)
            e.wait_ge(s_s.h, s_s.v)
            for name in dbg:
                src, shp, dt = dbgsrc[name]
                s_sq.dma(e.dma_start(out=dbg_t[name][:, :], in_=src.ap()))
            e.wait_ge(s_sq.h, s_sq.v)

    st.close()
    return nc, dbg_t


def _host_prepare(inputs):
    import ml_dtypes
    bf = ml_dtypes.bfloat16
    x = np.ascontiguousarray(np.asarray(inputs["L0_attn_x"]).reshape(8192, D))
    xTf = np.ascontiguousarray(x.T)
    WqT = np.ascontiguousarray(np.asarray(inputs["L0_attn_Wq"]).T)
    WkT = np.ascontiguousarray(np.asarray(inputs["L0_attn_Wk"]).T)
    WvT = np.ascontiguousarray(np.asarray(inputs["L0_attn_Wv"]).T)
    WoT = np.ascontiguousarray(np.asarray(inputs["L0_attn_Wo"]).T)
    wguT = np.ascontiguousarray(np.asarray(inputs["L0_ffn_w_gate_up"]).T)
    wprT = np.ascontiguousarray(np.asarray(inputs["L0_ffn_w_proj"]).T)
    wmod = np.asarray(inputs["L0_wmod"])
    ccb = np.ascontiguousarray(np.asarray(inputs["L0_c"]).reshape(D, 1))
    nwc = np.ascontiguousarray(np.asarray(inputs["L0_attn_norm_w"]).reshape(D, 1))

    U = np.triu(np.ones((64, 64), np.float32))
    u64d = np.zeros((128, 128), np.float32)
    u64d[:64, :64] = U
    u64d[64:, 64:] = U
    e2f = np.zeros((2, 128), np.float32)
    e2f[0, :64] = 1
    e2f[1, 64:] = 1
    e2c = (e2f * np.float32(C_THR)).astype(bf)
    e2b = e2f.astype(bf)
    g64f = np.zeros((128, 2), np.float32)
    g64f[:64, 0] = 1
    g64f[64:, 1] = 1
    mask1 = -np.triu(np.ones((64, 64), np.float32))
    masktn = np.zeros((128, 512), np.float32)
    masktn[:64] = np.tile(mask1, (1, 8))
    masktn[64:] = np.tile(mask1, (1, 8))

    in_maps = []
    for r in range(NCORE):
        xsl = np.zeros((D, TT), np.float32)
        lo = 1024 * r
        xsl[:, TW:] = xTf[:, lo:lo + T]
        wm = np.ones((128, 1), np.float32)
        if r > 0:
            xsl[:, :TW] = xTf[:, lo - TW:lo]
        else:
            wm[:] = 0.0
        in_maps.append({
            "xT": np.ascontiguousarray(xsl),
            "WqT": WqT, "WkT": WkT, "WvT": WvT, "WoT": WoT,
            "nwc": nwc, "cc": ccb,
            "wmod_sl": np.ascontiguousarray(wmod[:, MODSH * r:MODSH * (r + 1)]),
            "wguT_sl": np.ascontiguousarray(wguT[GUSH * r:GUSH * (r + 1)]),
            "wprT_sl": np.ascontiguousarray(wprT[PRSH * r:PRSH * (r + 1)]),
            "wmask": wm,
            "u64d": u64d, "e2c": e2c, "e2b": e2b, "g64f": g64f, "g64b": g64f.astype(bf),
            "ones128b": np.ones((128, 1), bf), "ones1x128": np.ones((1, 128), np.float32),
            "identf": np.eye(128, dtype=np.float32), "identb": np.eye(128, dtype=np.float32).astype(bf),
            "masktn": masktn.astype(bf),
            "epsc": np.tile(np.array([[LN_EPS, RMS_EPS, FFN_EPS, 0.0]], np.float32), (128, 1)),
        })
    return in_maps


def kernel(**inputs):
    in_maps = _host_prepare(inputs)
    nc, _ = build(DBG, STAGE)
    res = run_bass_kernel_spmd(nc, in_maps, core_ids=list(range(NCORE)))
    outs = [r["out_fm"] for r in res.results]
    full = np.concatenate([o.T for o in outs], axis=0)
    return full.reshape(1, 8192, D).astype(np.float32)
